# version 09
# baseline (speedup 1.0000x reference)
import numpy as np
import ml_dtypes
import concourse.bass as bass
import concourse.mybir as mybir
from concourse.bass_utils import run_bass_kernel_spmd
from contextlib import ExitStack
from concourse.bass_core_rust import IndirectOffsetOnAxis

F32 = mybir.dt.float32
BF16 = mybir.dt.bfloat16
I32 = mybir.dt.int32
AX = mybir.AxisListType
ALU = mybir.AluOpType
AF = mybir.ActivationFunctionType
bfd = ml_dtypes.bfloat16

D = 2048
KD = 16
TH = 1024
E = 16
NH = 16
KVH = 4
DH = 128
HALF = 64
SCL = 1.0 / float(np.sqrt(np.float32(DH)))
NEG = -1.0e30
EPS = 1e-5
C = 384
CT = 3


class Res:
    __slots__ = ("wl", "rs", "name")
    def __init__(self, name=""):
        self.wl = []
        self.rs = []
        self.name = name


def fork(parent, children):
    for c in children:
        c.wl = list(parent.wl)
        c.rs = list(parent.rs)


def merge(parent, children):
    for c in children:
        parent.wl.extend(c.wl)
        parent.rs.extend(c.rs)


class Bld:
    ENGS = ("t", "v", "s", "g", "y")

    def __init__(self, nc, es):
        self.nc = nc
        self.ops = {e: [] for e in self.ENGS}
        self.sems = {}
        self.cnt = {}
        for name in ("t", "v", "s", "g", "I", "R0", "R1", "R2", "R3",
                     "S0", "S1", "Q0", "Q1", "KV", "OUT", "XT",
                     "G0", "G1", "G2"):
            self.sems[name] = es.enter_context(nc.semaphore("sem_" + name))
            self.cnt[name] = 0
        self.hw = {e: {} for e in self.ENGS}

    def op(self, eng, fn, reads=(), writes=(), psum_reads=(), precise=(),
           extra=(), dma=None):
        toks = []
        for r in reads:
            toks.extend(r.wl)
        for r in writes:
            toks.extend(r.wl)
            toks.extend(r.rs)
        for r in psum_reads:
            toks.extend(r.wl)
            if eng in ("v", "s"):
                other = "s" if eng == "v" else "v"
                toks.extend(t for t in r.rs if t[0] == other)
        toks.extend(extra)
        need = {}
        for se, val in toks:
            if se == eng == "t":
                continue
            if val > need.get(se, 0):
                need[se] = val
        for r in precise:
            for se, val in r.wl:
                if val > need.get(se, 0):
                    need[se] = val
        hw = self.hw[eng]
        need = {se: v for se, v in need.items() if v > hw.get(se, 0)}
        for se, v in need.items():
            hw[se] = v
        if dma:
            sname = dma
            self.cnt[sname] += 16
            tok = (sname, self.cnt[sname])
        elif fn is not None:
            self.cnt[eng] += 1
            tok = (eng, self.cnt[eng])
        else:
            tok = None
        sem = self.sems[tok[0]] if tok else None
        inc = 16 if dma else 1
        nd = sorted(need.items())
        sems = self.sems

        def thunk(h, nd=nd, fn=fn, sem=sem, inc=inc):
            for se, v in nd:
                h.wait_ge(sems[se], v)
            if fn is not None:
                ins = fn(h)
                if sem is not None:
                    ins.then_inc(sem, inc)

        self.ops[eng].append(thunk)
        if tok:
            for r in reads:
                r.rs.append(tok)
            for r in psum_reads:
                r.rs.append(tok)
            for r in precise:
                r.rs.append(tok)
            for r in writes:
                r.wl = [tok]
                r.rs = []
        return tok


_DBG = None


def build_program(dbg=None):
    nc = bass.Bass("TRN2", target_bir_lowering=False, debug=False)

    xo_d = nc.dram_tensor("xo", [128, KD, TH], F32, kind="ExternalInput").ap()
    xp_d = nc.dram_tensor("xp", [128, KD, TH], F32, kind="ExternalInput").ap()
    sin_d = nc.dram_tensor("sin", [128, 2 * TH], BF16, kind="ExternalInput").ap()
    cos_d = nc.dram_tensor("cos", [128, 2 * TH], BF16, kind="ExternalInput").ap()
    c01_d = nc.dram_tensor("c01", [128, 1], F32, kind="ExternalInput").ap()
    wq_d = nc.dram_tensor("wq", [NH, 128, KD, 128], F32, kind="ExternalInput").ap()
    wk_d = nc.dram_tensor("wk", [128, KD, 512], F32, kind="ExternalInput").ap()
    wv_d = nc.dram_tensor("wv", [128, KD, 512], F32, kind="ExternalInput").ap()
    wo_d = nc.dram_tensor("wo", [KD, 128, KD, 128], F32, kind="ExternalInput").ap()
    wr_d = nc.dram_tensor("wr", [128, KD, E], F32, kind="ExternalInput").ap()
    rb_d = nc.dram_tensor("rb", [1, E], F32, kind="ExternalInput").ap()
    wgu_d = nc.dram_tensor("wgu", [E, 128, 16, 2, 16, 128], F32,
                           kind="ExternalInput").ap()
    bgu_d = nc.dram_tensor("bgu", [128, E, 32], F32, kind="ExternalInput").ap()
    wd_d = nc.dram_tensor("wd", [E, 128, 8, 16, 256], F32,
                          kind="ExternalInput").ap()
    bd_d = nc.dram_tensor("bd", [E, D], F32, kind="ExternalInput").ap()
    out_d = nc.dram_tensor("out", [128, KD, TH], F32, kind="ExternalOutput").ap()
    kv_scr = nc.dram_tensor("kvscr", [2, 128, 8192], BF16, kind="Internal").ap()
    xtok_d = nc.dram_tensor("xtokd", [8 * 128, D], BF16, kind="Internal").ap()
    wcache_a = nc.dram_tensor("wcachea", [E // 2, 24, 128, 4096], BF16,
                              kind="Internal").ap()
    wcache_b = nc.dram_tensor("wcacheb", [E // 2, 24, 128, 4096], BF16,
                              kind="Internal").ap()
    wcache = lambda e, u: (wcache_a if e < E // 2 else wcache_b)[e % (E // 2), u]

    ident_c = nc.inline_tensor(np.eye(128, dtype=bfd), "identc").ap()
    ones128_c = nc.inline_tensor(np.ones((128, 128), dtype=bfd), "ones128c").ap()
    ones1_c = nc.inline_tensor(np.ones((1, 128), dtype=bfd), "ones1c").ap()
    sel_np = np.zeros((E, E * 128), dtype=bfd)
    for e in range(E):
        sel_np[e, e * 128:(e + 1) * 128] = 1.0
    sel_c = nc.inline_tensor(sel_np, "selc").ap()
    tri_np = np.zeros((128, 4 * 512), np.float32)
    for s in range(4):
        for p in range(128):
            tri_np[p, s * 512:(s + 1) * 512][np.arange(512) > s * 128 + p] = NEG
    tri_c = nc.inline_tensor(tri_np.astype(bfd), "tric").ap()
    eps_c = nc.inline_tensor(np.full((128, 1), EPS, np.float32), "epsc").ap()
    tid = np.arange(1024).reshape(8, 128).T
    ioth_c = nc.inline_tensor((tid // 4).astype(bfd), "iothc").ap()
    iotl_c = nc.inline_tensor((tid % 4).astype(bfd), "iotlc").ap()
    iot_np = np.tile(np.arange(C, dtype=np.float32)[None, :], (128, 1))
    iot_c = nc.inline_tensor(iot_np, "iotc").ap()
    ult_np = np.zeros((128, 128), dtype=bfd)
    for k in range(128):
        ult_np[k, k + 1:] = 1.0
    ult_c = nc.inline_tensor(ult_np, "ultc").ap()
    swap_np = np.zeros((128, 128), dtype=bfd)
    for k in range(128):
        swap_np[k, (k + 64) % 128] = 1.0
    swap_c = nc.inline_tensor(swap_np, "swapc").ap()

    es = ExitStack()
    with es:
        sb = lambda name, shape, dt: es.enter_context(
            nc.sbuf_tensor(name, shape, dt))
        xo = sb("xo_s", [128, KD * TH], BF16)
        xp = sb("xp_s", [128, KD * TH], BF16)
        xn = sb("xn_s", [128, KD * TH], BF16)
        am = sb("am_s", [128, KD * TH], BF16)
        rg = [sb(f"rg{i}", [128, 4096], BF16) for i in range(2)]
        xg = sb("xg_s", [128, KD * C], BF16)
        scr = sb("scr_s", [128, 3584], F32)
        scb = sb("scb_s", [128, 2048], BF16)
        ident = sb("ident_s", [128, 128], BF16)
        swapb = sb("swap_s", [128, 128], BF16)
        iotb = sb("iot_s", [128, C], F32)
        iothb = sb("ioth_s", [128, 8], BF16)
        iotlb = sb("iotl_s", [128, 8], BF16)
        idxf = sb("idxf_s", [128, 2], F32)
        idxi = sb("idxi_s", [128, CT * 2], I32)
        ultb = sb("ult_s", [128, 128], BF16)
        Mb_sb = sb("Mb_s", [128, 128], BF16)
        rankm = sb("rankm_s", [128, 128], F32)
        Shtok = sb("Shtok_s", [128, 128], F32)
        ones128 = sb("ones128_s", [128, 128], BF16)
        ones1 = sb("ones1_s", [1, 128], BF16)
        sel = sb("sel_s", [E, E * 128], BF16)
        tri = sb("tri_s", [128, 4 * 512], BF16)
        epsc = sb("eps_s", [128, 1], F32)
        sinb = sb("sin_s", [128, 2 * TH], BF16)
        cosb = sb("cos_s", [128, 2 * TH], BF16)
        c01 = sb("c01_s", [128, 1], F32)
        wr_sb = sb("wr_sb", [128, KD * E], BF16)
        rb_sb = sb("rb_sb", [1, E], BF16)
        bgu_sb = sb("bgu_sb", [128, E * 32], F32)
        bd_sb = sb("bd_sb", [E, D], BF16)
        lg = sb("lg_s", [128, E], F32)
        m4 = sb("m4_s", [128, 4], F32)
        nm0 = sb("nm0_s", [128, 1], F32)
        e4 = sb("e4_s", [128, 4], F32)
        s4 = sb("s4_s", [128, 1], F32)
        rs4 = sb("rs4_s", [128, 1], F32)
        w4 = sb("w4_s", [128, 4], F32)
        msk4 = sb("msk4_s", [128, 4 * E], F32)
        Sll = sb("S_s", [128, E], F32)
        Shbf = sb("Shbf_s", [128, E], BF16)
        ShT = sb("ShT_s", [E, TH], BF16)
        sums4 = sb("sums4_s", [128, 4], F32)
        sumt = sb("sumt_s", [128, 1], F32)
        rst = sb("rst_s", [128, 1], F32)
        Dt = [sb(f"Dt{i}", [128, 128], BF16) for i in range(2)]
        k2m = sb("k2m_s", [1, KVH], F32)
        k2c = sb("k2c_s", [1, 1], F32)
        q2m = sb("q2m_s", [1, 1], F32)
        b2 = sb("b2_s", [1, 1], F32)
        lnb2 = sb("lnb2_s", [1, 1], F32)
        bsq = sb("bsq_s", [1, 1], F32)
        nbv = sb("nbv_s", [1, 1], F32)
        nbb = sb("nbb_s", [1, 1], BF16)
        bc = sb("bc_s", [128, 1], F32)
        bc01 = sb("bc01_s", [128, 1], F32)

        ps = [es.enter_context(nc.psum_tensor(f"ps{i}", [128, 512], F32))
              for i in range(8)]

        b = Bld(nc, es)

        r_xo, r_xp, r_xn, r_am = Res("xo"), Res("xp"), Res("xn"), Res("am")
        r_rg = [Res(f"rg{i}") for i in range(2)]
        r_scr, r_scb = Res("scr"), Res("scb")
        r_ps = [Res(f"ps{i}") for i in range(8)]
        r_consts = Res("consts")
        r_xg = Res("xg")
        r_wc = Res("wcache")
        r_Mb, r_rankm, r_Shtok = Res("Mb"), Res("rankm"), Res("Shtok")
        r_idxf, r_idxi, r_xtokd = Res("idxf"), Res("idxi"), Res("xtokd")
        r_small = Res("smallw")
        r_lg, r_m4, r_nm0, r_e4 = Res("lg"), Res("m4"), Res("nm0"), Res("e4")
        r_s4, r_rs4, r_w4, r_msk4 = Res("s4"), Res("rs4"), Res("w4"), Res("msk4")
        r_S, r_Shbf, r_ShT = Res("S"), Res("Shbf"), Res("ShT")
        r_sums4, r_sumt, r_rst = Res("sums4"), Res("sumt"), Res("rst")
        r_Dt = [Res("Dt0"), Res("Dt1")]
        r_k2m, r_k2c, r_q2m, r_b2 = Res("k2m"), Res("k2c"), Res("q2m"), Res("b2")
        r_lnb2, r_bsq, r_nbv, r_nbb = Res("lnb2"), Res("bsq"), Res("nbv"), Res("nbb")
        r_bc, r_bc01 = Res("bc"), Res("bc01")
        r_kvscr = Res("kvscr")
        r_sc = Res("sincos")

        for dst, src, rr in ((ident, ident_c, r_consts),
                             (ones128, ones128_c, r_consts),
                             (ones1, ones1_c, r_consts),
                             (sel, sel_c, r_consts),
                             (tri, tri_c, r_consts),
                             (epsc, eps_c, r_consts),
                             (sinb, sin_d, r_sc),
                             (cosb, cos_d, r_sc),
                             (swapb, swap_c, r_consts),
                             (iotb, iot_c, r_consts),
                             (iothb, ioth_c, r_consts),
                             (iotlb, iotl_c, r_consts),
                             (ultb, ult_c, r_consts),
                             (c01, c01_d, r_consts)):
            b.op("y", lambda h, dst=dst, src=src: h.dma_start(dst[:], src),
                 writes=(rr,), dma="I")
        b.op("y", lambda h: h.dma_start(
            bgu_sb[:].rearrange("p (e s) -> p e s", e=E), bgu_d),
            writes=(r_small,), dma="I")
        b.op("g", lambda h: h.dma_start(
            wr_sb[:].rearrange("p (k e) -> p k e", k=KD), wr_d),
            writes=(r_small,), dma="I")
        b.op("g", lambda h: h.dma_start(rb_sb[:], rb_d), writes=(r_small,),
             dma="I")
        b.op("g", lambda h: h.dma_start(bd_sb[:], bd_d), writes=(r_small,),
             dma="I")
        b.op("g", lambda h: h.dma_start(
            xp[:].rearrange("p (k t) -> p k t", k=KD), xp_d),
            writes=(r_xp,), dma="I")
        b.op("g", lambda h: h.dma_start(
            xo[:].rearrange("p (k t) -> p k t", k=KD), xo_d),
            writes=(r_xo,), dma="I")
        for rr in (r_consts, r_sc, r_small, r_xp, r_xo):
            rr.wl = [("I", b.cnt["I"])]

        def MM(pbank, out_ap, lhsT_ap, rhs_ap, start, stop, reads=()):
            return b.op("t", lambda h, o=out_ap, l=lhsT_ap, r=rhs_ap,
                        st=start, sp=stop: h.matmul(o, l, r, start=st, stop=sp),
                        reads=reads, writes=(r_ps[pbank],))

        def emit_ln(xbuf, r_xbuf):
            mean = scr[:, 0:512]
            rstd = scr[:, 512:1024]
            tmp = scr[:, 1024:1536]
            for tc in range(2):
                for k in range(KD):
                    pp = k % 2
                    xsl = xbuf[:, k * TH + tc * 512:k * TH + (tc + 1) * 512]
                    xb_ap = scb[:, pp * 512:(pp + 1) * 512]
                    xq_ap = scb[:, 1024 + pp * 512:1024 + (pp + 1) * 512]
                    b.op("v", lambda h, a=xb_ap, x=xsl: h.tensor_copy(a, x),
                         reads=(r_xbuf,), writes=(r_scb,))
                    b.op("s", lambda h, a=xq_ap, x=xb_ap: h.activation(
                        a, x, AF.Square), reads=(r_scb,), writes=(r_scb,))
                    MM(6, ps[6][:], ones128[:], xb_ap,
                       start=(k == 0), stop=(k == KD - 1),
                       reads=(r_scb, r_consts))
                    MM(7, ps[7][:], ones128[:], xq_ap,
                       start=(k == 0), stop=(k == KD - 1),
                       reads=(r_scb, r_consts))
                b.op("v", lambda h: h.tensor_scalar_mul(mean, ps[6][:], 1.0 / D),
                     psum_reads=(r_ps[6],), writes=(r_scr,))
                b.op("v", lambda h: h.tensor_mul(tmp, mean, mean),
                     reads=(r_scr,), writes=(r_scr,))
                b.op("v", lambda h: h.tensor_scalar_mul(rstd, ps[7][:], 1.0 / D),
                     psum_reads=(r_ps[7],), writes=(r_scr,))
                b.op("v", lambda h: h.tensor_sub(rstd, rstd, tmp),
                     reads=(r_scr,), writes=(r_scr,))
                b.op("s", lambda h: h.activation(rstd, rstd, AF.Ln, bias=epsc[:]),
                     reads=(r_scr, r_consts), writes=(r_scr,))
                b.op("s", lambda h: h.activation(rstd, rstd, AF.Exp, scale=-0.5),
                     reads=(r_scr,), writes=(r_scr,))
                for k in range(KD):
                    xsl = xbuf[:, k * TH + tc * 512:k * TH + (tc + 1) * 512]
                    nsl = xn[:, k * TH + tc * 512:k * TH + (tc + 1) * 512]
                    b.op("v", lambda h, x=xsl: h.tensor_sub(tmp, x, mean),
                         reads=(r_xbuf, r_scr), writes=(r_scr,))
                    b.op("v", lambda h, n=nsl: h.tensor_mul(n, tmp, rstd),
                         reads=(r_scr,), writes=(r_xn,))

        def emit_router(limit="full"):
            for tt in range(8):
                pb = 6 + (tt % 2)
                lga = ps[pb][:, 0:E]
                for k in range(KD):
                    MM(pb, lga,
                       xn[:, k * TH + tt * 128:k * TH + (tt + 1) * 128],
                       wr_sb[:, k * E:(k + 1) * E],
                       start=(k == 0), stop=False, reads=(r_xn, r_small))
                MM(pb, lga, ones1[:], rb_sb[:], start=False, stop=True,
                   reads=(r_consts, r_small))
                b.op("v", lambda h, a=lga: h.tensor_copy(lg[:], a),
                     psum_reads=(r_ps[pb],), writes=(r_lg,))
                if limit == "lg":
                    continue
                for j in range(4):
                    b.op("v", lambda h, j=j: h.reduce_max(
                        m4[:, j:j + 1], lg[:], axis=AX.X),
                        reads=(r_lg,), writes=(r_m4,))
                    b.op("v", lambda h, j=j: h.tensor_scalar(
                        msk4[:, j * E:(j + 1) * E], lg[:], m4[:, j:j + 1],
                        None, ALU.is_ge),
                        reads=(r_lg,), precise=(r_m4,), writes=(r_msk4,))
                    if j < 3:
                        b.op("v", lambda h, j=j: h.scalar_tensor_tensor(
                            lg[:], msk4[:, j * E:(j + 1) * E], NEG, lg[:],
                            ALU.mult, ALU.add),
                            reads=(r_msk4, r_lg), writes=(r_lg,))
                b.op("v", lambda h: h.tensor_scalar_mul(nm0[:], m4[:, 0:1], -1.0),
                     reads=(r_m4,), writes=(r_nm0,))
                b.op("s", lambda h: h.activation(e4[:], m4[:], AF.Exp,
                                                 bias=nm0[:]),
                     reads=(r_m4, r_nm0), writes=(r_e4,))
                b.op("v", lambda h: h.reduce_sum(s4[:], e4[:], axis=AX.X),
                     reads=(r_e4,), writes=(r_s4,))
                b.op("v", lambda h: h.reciprocal(rs4[:], s4[:]),
                     reads=(r_s4,), writes=(r_rs4,))
                b.op("v", lambda h: h.tensor_scalar(
                    w4[:], e4[:], rs4[:, 0:1], None, ALU.mult),
                    reads=(r_e4,), precise=(r_rs4,), writes=(r_w4,))
                if limit == "top4":
                    continue
                b.op("v", lambda h: h.tensor_scalar(
                    Sll[:], msk4[:, 0:E], w4[:, 0:1], None, ALU.mult),
                    reads=(r_msk4,), precise=(r_w4,), writes=(r_S,))
                for j in range(1, 4):
                    b.op("v", lambda h, j=j: h.scalar_tensor_tensor(
                        Sll[:], msk4[:, j * E:(j + 1) * E], w4[:, j:j + 1],
                        Sll[:], ALU.mult, ALU.add),
                        reads=(r_msk4, r_S), precise=(r_w4,), writes=(r_S,))
                if limit == "sacc":
                    continue
                b.op("v", lambda h: h.tensor_scalar_mul(Shbf[:], Sll[:], 0.5),
                     reads=(r_S,), writes=(r_Shbf,))
                if limit == "shbf":
                    continue
                MM(pb, ps[pb][0:E, 0:128], Shbf[:], ident[:],
                   start=True, stop=True, reads=(r_Shbf, r_consts))
                if limit == "tpose":
                    continue
                b.op("v", lambda h, tt=tt, pb=pb: h.tensor_copy(
                    ShT[:, tt * 128:(tt + 1) * 128], ps[pb][0:E, 0:128]),
                    psum_reads=(r_ps[pb],), writes=(r_ShT,))
                if limit == "sht":
                    continue
                b.op("v", lambda h: h.tensor_add(lg[:], msk4[:, 0:E],
                                                 msk4[:, E:2 * E]),
                     reads=(r_msk4,), writes=(r_lg,))
                b.op("v", lambda h: h.tensor_add(lg[:], lg[:],
                                                 msk4[:, 2 * E:3 * E]),
                     reads=(r_msk4, r_lg), writes=(r_lg,))
                b.op("v", lambda h: h.tensor_add(lg[:], lg[:],
                                                 msk4[:, 3 * E:4 * E]),
                     reads=(r_msk4, r_lg), writes=(r_lg,))
                b.op("v", lambda h, tt=tt: h.tensor_copy(
                    Mb_sb[:, tt * E:(tt + 1) * E], lg[:]),
                    reads=(r_lg,), writes=(r_Mb,))
                b.op("v", lambda h, tt=tt: h.tensor_scalar_mul(
                    Shtok[:, tt * E:(tt + 1) * E], Sll[:], 0.5),
                    reads=(r_S,), writes=(r_Shtok,))
            if limit != "full":
                return
            for tt in range(8):
                pb = 6 + (tt % 2)
                rka = ps[pb][:, 0:E]
                first = True
                for t2 in range(tt):
                    MM(pb, rka, ones128[:], Mb_sb[:, t2 * E:(t2 + 1) * E],
                       start=first, stop=False, reads=(r_consts, r_Mb))
                    first = False
                MM(pb, rka, ultb[:], Mb_sb[:, tt * E:(tt + 1) * E],
                   start=first, stop=True, reads=(r_consts, r_Mb))
                b.op("v", lambda h, rka=rka: h.tensor_scalar_add(
                    lg[:], rka, 1.0), psum_reads=(r_ps[pb],), writes=(r_lg,))
                b.op("v", lambda h, tt=tt: h.tensor_tensor(
                    lg[:], lg[:], Mb_sb[:, tt * E:(tt + 1) * E], ALU.mult),
                    reads=(r_lg, r_Mb), writes=(r_lg,))
                b.op("v", lambda h, tt=tt: h.tensor_scalar(
                    rankm[:, tt * E:(tt + 1) * E], lg[:], -1.0, None, ALU.add),
                    reads=(r_lg,), writes=(r_rankm,))

        def emit_experts_sparse(xbuf, r_xbuf, sweep):
            r_act, r_Gc, r_gtb, r_gts = (Res("act"), Res("Gc"), Res("gtb"),
                                         Res("gts"))
            fork(r_xn, [r_act, r_Gc, r_gtb, r_gts])
            r_stg = Res("stg")
            r_gslot = [Res("g0"), Res("g1"), Res("g2")]
            r_rge = [Res("rge0"), Res("rge1")]
            fork(r_am, [r_stg] + r_gslot + r_rge)
            act = xn
            Gc = xn
            gtb = xn
            gts = [xn[:, 6528:6528 + C], xn[:, 6912:6912 + C]]
            rgx = [rg[0][:], rg[1][:], am[:, 8192:12288], am[:, 12288:16384]]
            r_rgx = [r_rg[0], r_rg[1], r_rge[0], r_rge[1]]
            gsl_ap = [am[:, 2048 + i * 2048:2048 + (i + 1) * 2048]
                      for i in range(3)]

            for tt in range(8):
                pb2 = 2 + (tt % 2)
                so = (tt % 2) * 2048
                r_so = r_stg if tt % 2 == 0 else r_gslot[0]
                for dsl in range(KD):
                    MM(pb2, ps[pb2][:, (dsl % 4) * 128:(dsl % 4) * 128 + 128],
                       xn[:, dsl * TH + tt * 128:dsl * TH + (tt + 1) * 128],
                       ident[:], start=True, stop=True,
                       reads=(r_xn, r_consts))
                    if dsl % 4 == 3:
                        b.op("v", lambda h, dsl=dsl, pb2=pb2, so=so:
                             h.tensor_copy(
                                 am[:, so + (dsl - 3) * 128:
                                    so + (dsl + 1) * 128],
                                 ps[pb2][:]),
                             psum_reads=(r_ps[pb2],), writes=(r_so,))
                b.op("y", lambda h, tt=tt, so=so: h.dma_start(
                    xtok_d[tt * 128:(tt + 1) * 128, :],
                    am[:, so:so + 2048]),
                    reads=(r_so,), writes=(r_xtokd,), dma="XT")
            r_xtokd.wl = [("XT", b.cnt["XT"])]

            def build_gt(e):
                for tt in range(8):
                    rcol = rankm[:, tt * E + e:tt * E + e + 1]
                    b.op("v", lambda h, tt=tt, rcol=rcol: h.tensor_scalar(
                        gtb[:, 12288 + tt * C:12288 + (tt + 1) * C],
                        iotb[:], rcol, None, ALU.is_equal),
                        reads=(r_consts,), precise=(r_rankm,), writes=(r_gtb,))

            def build_idx(e, par):
                for ct in range(CT):
                    for tt in range(8):
                        MM(6, ps[6][:, 0:1],
                           gtb[:, 12288 + tt * C + ct * 128:
                               12288 + tt * C + (ct + 1) * 128],
                           iothb[:, tt:tt + 1], start=(tt == 0), stop=False,
                           reads=(r_gtb, r_consts))
                    for tt in range(8):
                        MM(6, ps[6][:, 1:2],
                           gtb[:, 12288 + tt * C + ct * 128:
                               12288 + tt * C + (ct + 1) * 128],
                           iotlb[:, tt:tt + 1], start=(tt == 0),
                           stop=(tt == 7), reads=(r_gtb, r_consts))
                    b.op("v", lambda h: h.tensor_copy(
                        idxf[:, 1:2], ps[6][:, 0:1]),
                        psum_reads=(r_ps[6],), writes=(r_idxf,))
                    b.op("v", lambda h: h.scalar_tensor_tensor(
                        idxf[:, 0:1], idxf[:, 1:2], 4.0, ps[6][:, 1:2],
                        ALU.mult, ALU.add),
                        reads=(r_idxf,), psum_reads=(r_ps[6],),
                        writes=(r_idxf,))
                    b.op("v", lambda h, ct=ct, par=par: h.tensor_copy(
                        idxi[:, par * CT + ct:par * CT + ct + 1],
                        idxf[:, 0:1]),
                        reads=(r_idxf,), writes=(r_idxi,))

            def build_gc(e):
                for ttg in range(2):
                    for tt4 in range(4):
                        tt = ttg * 4 + tt4
                        gsl = gts[tt % 2]
                        rcol = rankm[:, tt * E + e:tt * E + e + 1]
                        scol = Shtok[:, tt * E + e:tt * E + e + 1]
                        b.op("v", lambda h, gsl=gsl, rcol=rcol, scol=scol:
                             h.tensor_scalar(gsl, iotb[:], rcol, scol,
                                             ALU.is_equal, ALU.mult),
                             reads=(r_consts,), precise=(r_rankm, r_Shtok),
                             writes=(r_gts,))
                        for ct in range(CT):
                            MM(2 + ct, ps[2 + ct][:, tt4 * 128:(tt4 + 1) * 128],
                               gsl[:, ct * 128:(ct + 1) * 128], ident[:],
                               start=True, stop=True,
                               reads=(r_gts, r_consts))
                    for ct in range(CT):
                        b.op("v", lambda h, ct=ct, ttg=ttg: h.tensor_copy(
                            Gc[:, 9216 + ct * 1024 + ttg * 512:
                               9216 + ct * 1024 + (ttg + 1) * 512],
                            ps[2 + ct][:]),
                            psum_reads=(r_ps[2 + ct],), writes=(r_Gc,))

            gs = scr[:, 1536:1536 + C]
            sg = scr[:, 2048:2048 + C]
            gl = scr[:, 2560:2560 + C]
            u2 = scr[:, 3072:3072 + C]
            def emit_gathers(e):
                par = e % 2
                for ct in range(CT):
                    b.op("g", lambda h, ct=ct, par=par: h.indirect_dma_start(
                        gsl_ap[ct], None, xtok_d,
                        IndirectOffsetOnAxis(
                            idxi[:, par * CT + ct:par * CT + ct + 1], 0)),
                        reads=(r_idxi, r_xtokd), writes=(r_gslot[ct],),
                        dma=f"G{ct}")

            build_gt(0)
            build_idx(0, 0)
            emit_gathers(0)
            for e in range(E):
                par = e % 2
                for dsl in range(KD):
                    pb2 = dsl % 4
                    for ct in range(CT):
                        MM(pb2, ps[pb2][:, ct * 128:(ct + 1) * 128],
                           gsl_ap[ct][:, dsl * 128:(dsl + 1) * 128],
                           ident[:], start=True, stop=True,
                           reads=(r_gslot[ct], r_consts))
                    b.op("v", lambda h, dsl=dsl, pb2=pb2: h.tensor_copy(
                        xg[:, dsl * C:(dsl + 1) * C], ps[pb2][:, 0:C]),
                        psum_reads=(r_ps[pb2],), writes=(r_xg,))
                if e + 1 < E:
                    build_gt(e + 1)
                    build_idx(e + 1, 1 - par)
                build_gc(e)
                if e + 1 < E:
                    build_gt(e + 1)
                for jc in range(16):
                    slot = (e * 24 + jc) % 4
                    if sweep == 0:
                        b.op("g", lambda h, jc=jc, e=e, slot=slot: h.dma_start(
                            rgx[slot].rearrange("p (g k j) -> p g k j",
                                                g=2, k=16),
                            wgu_d[e, :, jc]), writes=(r_rgx[slot],),
                            dma=f"R{slot}")
                        b.op("y", lambda h, e=e, jc=jc, slot=slot: h.dma_start(
                            wcache(e, jc), rgx[slot]),
                            reads=(r_rgx[slot],), writes=(r_wc,),
                            dma=f"S{slot % 2}")
                    else:
                        b.op("y", lambda h, e=e, jc=jc, slot=slot: h.dma_start(
                            rgx[slot], wcache(e, jc)),
                            reads=(r_wc,), writes=(r_rgx[slot],),
                            dma=f"R{slot}")
                    da = jc
                    pg = jc % 2
                    pu = 2 + jc % 2
                    for k in range(KD):
                        MM(pg, ps[pg][:, 0:C],
                           rgx[slot][:, k * 128:(k + 1) * 128],
                           xg[:, k * C:(k + 1) * C],
                           start=(k == 0), stop=(k == KD - 1),
                           reads=(r_rgx[slot], r_xg))
                    for k in range(KD):
                        MM(pu, ps[pu][:, 0:C],
                           rgx[slot][:, 2048 + k * 128:2048 + (k + 1) * 128],
                           xg[:, k * C:(k + 1) * C],
                           start=(k == 0), stop=(k == KD - 1),
                           reads=(r_rgx[slot], r_xg))
                    gcol = bgu_sb[:, e * 32 + da:e * 32 + da + 1]
                    ucol = bgu_sb[:, e * 32 + 16 + da:e * 32 + 16 + da + 1]
                    b.op("v", lambda h, pg=pg, gc=gcol: h.tensor_scalar(
                        gs, ps[pg][:, 0:C], gc, 7.0, ALU.add, ALU.min),
                        psum_reads=(r_ps[pg],), reads=(r_small,),
                        writes=(r_scr,))
                    b.op("s", lambda h: h.activation(
                        sg, gs, AF.Sigmoid, scale=1.702),
                        reads=(r_scr,), writes=(r_scr,))
                    b.op("v", lambda h: h.tensor_mul(gl, gs, sg),
                         reads=(r_scr,), writes=(r_scr,))
                    b.op("v", lambda h, pu=pu, uc=ucol: h.tensor_scalar(
                        u2, ps[pu][:, 0:C], uc, 7.0, ALU.add, ALU.min),
                        psum_reads=(r_ps[pu],), reads=(r_small,),
                        writes=(r_scr,))
                    b.op("v", lambda h: h.tensor_scalar(
                        u2, u2, -7.0, 1.0, ALU.max, ALU.add),
                        reads=(r_scr,), writes=(r_scr,))
                    b.op("v", lambda h, da=da: h.tensor_mul(
                        act[:, da * C:(da + 1) * C], u2, gl),
                        reads=(r_scr,), writes=(r_act,))
                if e + 1 < E:
                    emit_gathers(e + 1)
                last = (e == E - 1)
                for hc in range(4):
                    for hh in range(2):
                        slot = (e * 24 + 16 + hc * 2 + hh) % 4
                        if sweep == 0:
                            b.op("g", lambda h, e=e, u=hc * 2 + hh, slot=slot:
                                 h.dma_start(
                                     rgx[slot].rearrange("p (k j) -> p k j",
                                                         k=16),
                                     wd_d[e, :, u]), writes=(r_rgx[slot],),
                                 dma=f"R{slot}")
                            b.op("y", lambda h, e=e, u=16 + hc * 2 + hh,
                                 slot=slot: h.dma_start(
                                     wcache(e, u), rgx[slot]),
                                 reads=(r_rgx[slot],), writes=(r_wc,),
                                 dma=f"S{slot % 2}")
                        else:
                            b.op("y", lambda h, e=e, u=16 + hc * 2 + hh,
                                 slot=slot: h.dma_start(
                                     rgx[slot], wcache(e, u)),
                                 reads=(r_wc,), writes=(r_rgx[slot],),
                                 dma=f"R{slot}")
                        for ct in range(CT):
                            pd = 4 + (ct + hh) % 2
                            for k in range(KD):
                                MM(pd, ps[pd][:, 0:256],
                                   act[:, k * C + ct * 128:
                                       k * C + (ct + 1) * 128],
                                   rgx[slot][:, k * 256:(k + 1) * 256],
                                   start=(k == 0), stop=(k == KD - 1),
                                   reads=(r_act, r_rgx[slot]))
                            b.op("v", lambda h, ct=ct, hh=hh, pd=pd:
                                 h.tensor_copy(
                                     scb[:, ct * 512 + hh * 256:
                                         ct * 512 + (hh + 1) * 256],
                                     ps[pd][:, 0:256]),
                                 psum_reads=(r_ps[pd],), writes=(r_scb,))
                    for ht4 in range(4):
                        ht = hc * 4 + ht4
                        for tc in range(2):
                            pd = 6 + (ht4 * 2 + tc) % 2
                            for ct in range(CT):
                                MM(pd, ps[pd][:],
                                   scb[:, ct * 512 + ht4 * 128:
                                       ct * 512 + (ht4 + 1) * 128],
                                   Gc[:, 9216 + ct * 1024 + tc * 512:
                                      9216 + ct * 1024 + (tc + 1) * 512],
                                   start=(ct == 0),
                                   stop=(ct == CT - 1 and not last),
                                   reads=(r_scb, r_Gc))
                            if last:
                                MM(pd, ps[pd][:],
                                   bd_sb[0:E, ht * 128:(ht + 1) * 128],
                                   ShT[:, tc * 512:(tc + 1) * 512],
                                   start=False, stop=True,
                                   reads=(r_small, r_ShT))
                            xsl = xbuf[:, ht * TH + tc * 512:
                                       ht * TH + (tc + 1) * 512]
                            b.op("v", lambda h, x=xsl, pd=pd: h.tensor_tensor(
                                x, x, ps[pd][:], ALU.add),
                                reads=(r_xbuf,), psum_reads=(r_ps[pd],),
                                writes=(r_xbuf,))
            merge(r_xn, [r_act, r_Gc, r_gtb, r_gts])
            merge(r_am, [r_stg] + r_gslot + r_rge)

        def emit_experts(xbuf, r_xbuf):
            gs = scr[:, 1536:2048]
            sg = scr[:, 2048:2560]
            gl = scr[:, 2560:3072]
            u2 = scr[:, 3072:3584]
            for e in range(E):
                for tc in range(2):
                    MM(4 + tc, ps[4 + tc][:], sel[:, e * 128:(e + 1) * 128],
                       ShT[:, tc * 512:(tc + 1) * 512], start=True, stop=True,
                       reads=(r_consts, r_ShT))
                for jc in range(8):
                    slot = (e * 12 + jc) % 2
                    b.op("g", lambda h, e=e, jc=jc, slot=slot: h.dma_start(
                        rg[slot][:].rearrange("p (g k j) -> p g k j", g=2, k=16),
                        wgu_d[e, :, jc]), writes=(r_rg[slot],), dma="W")
                    for js in range(2):
                        da = jc * 2 + js
                        for tc in range(2):
                            pg = (da * 2 + tc) % 2
                            pu = 2 + (da * 2 + tc) % 2
                            xnsl = [xn[:, k * TH + tc * 512:
                                       k * TH + (tc + 1) * 512]
                                    for k in range(KD)]
                            for k in range(KD):
                                wsl = rg[slot][:, k * 256 + js * 128:
                                               k * 256 + js * 128 + 128]
                                MM(pg, ps[pg][:], wsl, xnsl[k],
                                   start=(k == 0), stop=(k == KD - 1),
                                   reads=(r_rg[slot], r_xn))
                            for k in range(KD):
                                wsl = rg[slot][:, 4096 + k * 256 + js * 128:
                                               4096 + k * 256 + js * 128 + 128]
                                MM(pu, ps[pu][:], wsl, xnsl[k],
                                   start=(k == 0), stop=(k == KD - 1),
                                   reads=(r_rg[slot], r_xn))
                            gcol = bgu_sb[:, e * 32 + da:e * 32 + da + 1]
                            ucol = bgu_sb[:, e * 32 + 16 + da:
                                          e * 32 + 16 + da + 1]
                            b.op("v", lambda h, pg=pg, gc=gcol: h.tensor_scalar(
                                gs, ps[pg][:], gc, 7.0, ALU.add, ALU.min),
                                psum_reads=(r_ps[pg],), reads=(r_small,),
                                writes=(r_scr,))
                            b.op("s", lambda h: h.activation(
                                sg, gs, AF.Sigmoid, scale=1.702),
                                reads=(r_scr,), writes=(r_scr,))
                            b.op("v", lambda h: h.tensor_mul(gl, gs, sg),
                                 reads=(r_scr,), writes=(r_scr,))
                            b.op("v", lambda h, pu=pu, uc=ucol: h.tensor_scalar(
                                u2, ps[pu][:], uc, 7.0, ALU.add, ALU.min),
                                psum_reads=(r_ps[pu],), reads=(r_small,),
                                writes=(r_scr,))
                            b.op("v", lambda h: h.tensor_scalar(
                                u2, u2, -7.0, 1.0, ALU.max, ALU.add),
                                reads=(r_scr,), writes=(r_scr,))
                            b.op("v", lambda h, tc=tc: h.tensor_tensor(
                                gl, gl, ps[4 + tc][:], ALU.mult),
                                reads=(r_scr,), psum_reads=(r_ps[4 + tc],),
                                writes=(r_scr,))
                            asl = am[:, da * TH + tc * 512:
                                     da * TH + (tc + 1) * 512]
                            b.op("v", lambda h, a=asl: h.tensor_mul(a, u2, gl),
                                 reads=(r_scr,), writes=(r_am,))
                last = (e == E - 1)
                for hc in range(4):
                    slot = (e * 12 + 8 + hc) % 2
                    b.op("g", lambda h, e=e, hc=hc, slot=slot: h.dma_start(
                        rg[slot][:].rearrange("p (k j) -> p k j", k=16),
                        wd_d[e, :, hc]), writes=(r_rg[slot],), dma="W")
                    for hs in range(4):
                        ht = hc * 4 + hs
                        for tc in range(2):
                            pd = 6 + (hs * 2 + tc) % 2
                            for k in range(KD):
                                wsl = rg[slot][:, k * 512 + hs * 128:
                                               k * 512 + hs * 128 + 128]
                                MM(pd, ps[pd][:], wsl,
                                   am[:, k * TH + tc * 512:
                                      k * TH + (tc + 1) * 512],
                                   start=(k == 0),
                                   stop=(k == KD - 1 and not last),
                                   reads=(r_rg[slot], r_am))
                            if last:
                                MM(pd, ps[pd][:],
                                   bd_sb[0:E, ht * 128:(ht + 1) * 128],
                                   ShT[:, tc * 512:(tc + 1) * 512],
                                   start=False, stop=True,
                                   reads=(r_small, r_ShT))
                            xsl = xbuf[:, ht * TH + tc * 512:
                                       ht * TH + (tc + 1) * 512]
                            b.op("v", lambda h, x=xsl, pd=pd: h.tensor_tensor(
                                x, x, ps[pd][:], ALU.add),
                                reads=(r_xbuf,), psum_reads=(r_ps[pd],),
                                writes=(r_xbuf,))

        def emit_rope(pb, pb2, dst, t0, r_dst):
            qev = scb[:, 0:512]
            m1 = scr[:, 1536:2048]
            m2 = scr[:, 2048:2560]
            csl = cosb[:, t0:t0 + 512]
            ssl = sinb[:, t0:t0 + 512]
            b.op("v", lambda h: h.tensor_copy(qev, ps[pb][:]),
                 psum_reads=(r_ps[pb],), writes=(r_scb,))
            MM(pb2, ps[pb2][:], swapb[:], qev, start=True, stop=True,
               reads=(r_scb, r_consts))
            b.op("v", lambda h, c=csl: h.tensor_tensor(m1, qev, c, ALU.mult),
                 reads=(r_scb, r_sc), writes=(r_scr,))
            b.op("v", lambda h, s=ssl, pb2=pb2: h.tensor_tensor(
                m2, ps[pb2][:], s, ALU.mult),
                psum_reads=(r_ps[pb2],), reads=(r_sc,), writes=(r_scr,))
            b.op("v", lambda h, d=dst: h.tensor_add(d, m1, m2),
                 reads=(r_scr,), writes=(r_dst,))

        def emit_kv(half, pos0):
            kvst = am[:, 0:8192]
            r_kvst = Res("kvst")
            fork(r_am, [r_kvst])
            b.op("g", lambda h: h.dma_start(
                rg[0][:].rearrange("p (k j) -> p k j", k=8), wk_d[:, 0:8]),
                writes=(r_rg[0],), dma="R0")
            b.op("g", lambda h: h.dma_start(
                rg[1][:].rearrange("p (k j) -> p k j", k=8), wk_d[:, 8:16]),
                writes=(r_rg[1],), dma="R1")
            for kv in range(KVH):
                for tc in range(2):
                    pb = tc % 2
                    for k in range(KD):
                        rgk = rg[k // 8]
                        wsl = rgk[:, (k % 8) * 512 + kv * 128:
                                  (k % 8) * 512 + kv * 128 + 128]
                        MM(pb, ps[pb][:], wsl,
                           xn[:, k * TH + tc * 512:k * TH + (tc + 1) * 512],
                           start=(k == 0), stop=(k == KD - 1),
                           reads=(r_rg[k // 8], r_xn))
                    base = kv * 1024 + tc * 512
                    emit_rope(pb, 2 + (kv % 2), am[:, base:base + 512],
                              pos0 + tc * 512, r_kvst)
            b.op("g", lambda h: h.dma_start(
                rg[0][:].rearrange("p (k j) -> p k j", k=8), wv_d[:, 0:8]),
                writes=(r_rg[0],), dma="R0")
            b.op("g", lambda h: h.dma_start(
                rg[1][:].rearrange("p (k j) -> p k j", k=8), wv_d[:, 8:16]),
                writes=(r_rg[1],), dma="R1")
            for tt in range(8):
                pb = tt % 2
                for k in range(KD):
                    rgk = rg[k // 8]
                    MM(pb, ps[pb][:],
                       xn[:, k * TH + tt * 128:k * TH + (tt + 1) * 128],
                       rgk[:, (k % 8) * 512:(k % 8 + 1) * 512],
                       start=(k == 0), stop=(k == KD - 1),
                       reads=(r_rg[k // 8], r_xn))
                b.op("v", lambda h, tt=tt, pb=pb: h.tensor_copy(
                    am[:, 4096 + tt * 512:4096 + (tt + 1) * 512],
                    ps[pb][:]),
                    psum_reads=(r_ps[pb],), writes=(r_kvst,))
            b.op("y", lambda h, half=half: h.dma_start(kv_scr[half],
                                                       am[:, 0:8192]),
                 reads=(r_kvst,), writes=(r_kvscr,), dma="KV")
            merge(r_am, [r_kvst])

        def finish(buf, rbuf):
            b.op("g", lambda h: h.dma_start(
                out_d, buf[:].rearrange("p (k t) -> p k t", k=KD)),
                reads=(rbuf,), dma="OUT")
            b.op("g", None, extra=[("OUT", b.cnt["OUT"])])
            b.op("y", None, extra=[("OUT", b.cnt["OUT"])])
            emit_block()
            return nc

        def emit_block():
            with nc.Block() as blk:
                @blk.tensor
                def _(h):
                    for f in b.ops["t"]:
                        f(h)

                @blk.vector
                def _(h):
                    for f in b.ops["v"]:
                        f(h)

                @blk.scalar
                def _(h):
                    for f in b.ops["s"]:
                        f(h)

                @blk.gpsimd
                def _(h):
                    for f in b.ops["g"]:
                        f(h)

                @blk.sync
                def _(h):
                    for f in b.ops["y"]:
                        f(h)

        emit_ln(xp, r_xp)
        if dbg == "ln1":
            return finish(xn, r_xn)
        emit_router(dbg[3:] if dbg and dbg.startswith("rt_") else "full")
        if dbg and dbg.startswith("rt_"):
            return finish(xn, r_xn)
        if dbg == "rt":
            b.op("v", lambda h: h.tensor_copy(xn[0:E, 0:TH], ShT[:]),
                 reads=(r_ShT,), writes=(r_xn,))
            b.op("v", lambda h: h.tensor_copy(xn[:, 2048:2064], lg[:]),
                 reads=(r_lg,), writes=(r_xn,))
            b.op("v", lambda h: h.tensor_copy(xn[:, 2064:2068], m4[:]),
                 reads=(r_m4,), writes=(r_xn,))
            b.op("v", lambda h: h.tensor_copy(xn[:, 2068:2072], e4[:]),
                 reads=(r_e4,), writes=(r_xn,))
            b.op("v", lambda h: h.tensor_copy(xn[:, 2072:2076], w4[:]),
                 reads=(r_w4,), writes=(r_xn,))
            b.op("v", lambda h: h.tensor_copy(xn[:, 2076:2092], Sll[:]),
                 reads=(r_S,), writes=(r_xn,))
            b.op("v", lambda h: h.tensor_copy(xn[:, 2092:2156], msk4[:]),
                 reads=(r_msk4,), writes=(r_xn,))
            return finish(xn, r_xn)
        emit_experts_sparse(xp, r_xp, 0)
        r_wc.wl = [("S0", b.cnt["S0"]), ("S1", b.cnt["S1"])]
        if dbg == "ffn1p":
            return finish(xp, r_xp)
        emit_ln(xp, r_xp)
        emit_kv(0, 0)
        if dbg == "kv0":
            return finish(am, r_am)
        emit_ln(xo, r_xo)
        emit_router()
        emit_experts_sparse(xo, r_xo, 1)
        if dbg == "ffn1o":
            return finish(xo, r_xo)
        emit_ln(xo, r_xo)
        emit_kv(1, TH)

        r_vct = Res("vct")
        fork(r_am, [r_vct])

        def kc_ap(kv, col0, n):
            return rg[kv // 2][:, (kv % 2) * 2048 + col0:
                               (kv % 2) * 2048 + col0 + n]

        def vc_ap(tt, col0, n):
            if tt < 12:
                return xg[:, tt * 512 + col0:tt * 512 + col0 + n]
            return am[:, 12288 + (tt - 12) * 512 + col0:
                      12288 + (tt - 12) * 512 + col0 + n]

        r_vc = [r_xg, r_vct]
        for half in range(2):
            for kv in range(KVH):
                b.op("y", lambda h, half=half, kv=kv: h.dma_start(
                    kc_ap(kv, half * 1024, 1024),
                    kv_scr[half][:, kv * 1024:(kv + 1) * 1024]),
                    reads=(r_kvscr,), writes=(r_rg[kv // 2],), dma="KV")
            for tt2 in range(8):
                tt = half * 8 + tt2
                b.op("y", lambda h, half=half, tt=tt, tt2=tt2: h.dma_start(
                    vc_ap(tt, 0, 512),
                    kv_scr[half][:, 4096 + tt2 * 512:4096 + (tt2 + 1) * 512]),
                    reads=(r_kvscr,),
                    writes=(r_xg if tt < 12 else r_vct,), dma="KV")

        kvtot = [("KV", b.cnt["KV"])]
        for rr in (r_rg[0], r_rg[1], r_xg, r_vct):
            rr.wl = list(kvtot)
        ksq = am[:, 5120:7168]
        qsq = am[:, 7168:7680]
        wqa = [am[:, 8192:10240], am[:, 10240:12288]]
        pslot = [am[:, 0:2048], am[:, 2048:4096]]
        qslot = [am[:, 4096:4608], am[:, 4608:5120]]
        ptbuf = [xp[:, 0:2048], xp[:, 2048:4096]]
        ybuf = xp[:, 8192:16384]
        r_wqa = [Res("wqa0"), Res("wqa1")]
        r_pslot = [Res("p0"), Res("p1")]
        r_qslot = [Res("q0"), Res("q1")]
        r_ptbuf = [Res("pt0"), Res("pt1")]
        r_y = Res("y")
        fork(r_am, r_wqa + r_pslot + r_qslot)
        fork(r_xp, r_ptbuf + [r_y])

        for kv in range(KVH):
            b.op("s", lambda h, kv=kv: h.activation(
                ksq, kc_ap(kv, 0, 2048), AF.Square),
                reads=(r_rg[kv // 2],), writes=(r_am,))
            for c in range(4):
                MM(6, ps[6][:], ones128[:], ksq[:, c * 512:(c + 1) * 512],
                   start=True, stop=True, reads=(r_am, r_consts))
                dst = k2m[0:1, kv:kv + 1] if c == 0 else k2c[:]
                rdst = (r_k2m,) if c == 0 else (r_k2c,)
                b.op("v", lambda h, d=dst: h.reduce_max(
                    d, ps[6][0:1, :], axis=AX.X),
                    psum_reads=(r_ps[6],), writes=rdst)
                if c > 0:
                    b.op("v", lambda h, kv=kv: h.tensor_max(
                        k2m[0:1, kv:kv + 1], k2m[0:1, kv:kv + 1], k2c[:]),
                        reads=(r_k2c, r_k2m), writes=(r_k2m,))

        for tc in range(2):
            for hd in range(NH):
                kv = hd // 4
                ws = hd % 2
                b.op("g", lambda h, hd=hd, ws=ws: h.dma_start(
                    wqa[ws][:].rearrange("p (k j) -> p k j", k=KD), wq_d[hd]),
                    writes=(r_wqa[ws],), dma=f"Q{ws}")
                for k in range(KD):
                    MM(5, ps[5][:], wqa[ws][:, k * 128:(k + 1) * 128],
                       xn[:, k * TH + tc * 512:k * TH + (tc + 1) * 512],
                       start=(k == 0), stop=(k == KD - 1),
                       reads=(r_wqa[ws], r_xn))
                qs = qslot[hd % 2]
                r_qs = r_qslot[hd % 2]
                emit_rope(5, 2 + (hd % 2), qs, TH + tc * 512, r_qs)
                b.op("s", lambda h, q=qs: h.activation(qsq, q, AF.Square),
                     reads=(r_qs,), writes=(r_am,))
                MM(6, ps[6][:], ones128[:], qsq, start=True, stop=True,
                   reads=(r_am, r_consts))
                b.op("v", lambda h: h.reduce_max(q2m[:], ps[6][0:1, :],
                                                 axis=AX.X),
                     psum_reads=(r_ps[6],), writes=(r_q2m,))
                b.op("v", lambda h, kv=kv: h.tensor_mul(
                    b2[:], q2m[:], k2m[0:1, kv:kv + 1]),
                    reads=(r_q2m, r_k2m), writes=(r_b2,))
                b.op("s", lambda h: h.activation(lnb2[:], b2[:], AF.Ln),
                     reads=(r_b2,), writes=(r_lnb2,))
                b.op("s", lambda h: h.activation(bsq[:], lnb2[:], AF.Exp,
                                                 scale=0.5),
                     reads=(r_lnb2,), writes=(r_bsq,))
                b.op("v", lambda h: h.tensor_scalar_mul(nbv[:], bsq[:], -SCL),
                     reads=(r_bsq,), writes=(r_nbv,))
                b.op("v", lambda h: h.tensor_copy(nbb[:], nbv[:]),
                     reads=(r_nbv,), writes=(r_nbb,))
                MM(6, ps[6][:, 0:1], ones1[:], nbb[:], start=True, stop=True,
                   reads=(r_consts, r_nbb))
                b.op("v", lambda h: h.tensor_copy(bc[:], ps[6][:, 0:1]),
                     psum_reads=(r_ps[6],), writes=(r_bc,))
                b.op("v", lambda h: h.tensor_add(bc01[:], bc[:], c01[:]),
                     reads=(r_bc, r_consts), writes=(r_bc01,))
                for qt in range(4):
                    qtg = tc * 4 + qt
                    pcur = pslot[qt % 2]
                    r_pcur = r_pslot[qt % 2]
                    nkc = 4 if tc == 1 else 3
                    for kc in range(nkc):
                        pb = kc % 2
                        MM(pb, ps[pb][:], qs[:, qt * 128:(qt + 1) * 128],
                           kc_ap(kv, kc * 512, 512),
                           start=True, stop=True,
                           reads=(r_qs, r_rg[kv // 2]))
                        diag = (kc == 2 and qtg < 4) or (kc == 3 and qtg >= 4)
                        if diag:
                            sdx = qtg if qtg < 4 else qtg - 4
                            b.op("v", lambda h, pb=pb, s=sdx: h.tensor_tensor(
                                ps[pb][:], ps[pb][:],
                                tri[:, s * 512:(s + 1) * 512], ALU.add),
                                reads=(r_consts,), psum_reads=(r_ps[pb],),
                                writes=(r_ps[pb],))
                        bcol = bc01 if kc < 2 else bc
                        rbcol = r_bc01 if kc < 2 else r_bc
                        b.op("s", lambda h, pb=pb, kc=kc, pc=pcur, bco=bcol:
                             h.activation(pc[:, kc * 512:(kc + 1) * 512],
                                          ps[pb][:], AF.Exp, bias=bco[:],
                                          scale=SCL,
                                          accum_out=sums4[:, kc:kc + 1]),
                             psum_reads=(r_ps[pb],), reads=(rbcol,),
                             writes=(r_pcur, r_sums4))
                    b.op("v", lambda h: h.tensor_add(sumt[:], sums4[:, 0:1],
                                                     sums4[:, 1:2]),
                         reads=(r_sums4,), writes=(r_sumt,))
                    b.op("v", lambda h: h.tensor_add(sumt[:], sumt[:],
                                                     sums4[:, 2:3]),
                         reads=(r_sums4,), writes=(r_sumt,))
                    if nkc == 4:
                        b.op("v", lambda h: h.tensor_add(
                            sumt[:], sumt[:], sums4[:, 3:4]),
                            reads=(r_sums4,), writes=(r_sumt,))
                    b.op("v", lambda h: h.reciprocal(rst[:], sumt[:]),
                         reads=(r_sumt,), writes=(r_rst,))
                    dcur = Dt[qt % 2]
                    r_dcur = r_Dt[qt % 2]
                    b.op("v", lambda h, d=dcur: h.tensor_scalar(
                        d[:], ident[:], rst[:, 0:1], None, ALU.mult),
                        reads=(r_consts,), precise=(r_rst,), writes=(r_dcur,))
                    nks = 16 if tc == 1 else 12
                    ptc = ptbuf[qt % 2]
                    r_ptc = r_ptbuf[qt % 2]
                    for ksg in range(nks // 4):
                        pb = 2 + ksg % 2
                        for k4 in range(4):
                            ks = ksg * 4 + k4
                            MM(pb, ps[pb][:, k4 * 128:(k4 + 1) * 128],
                               pcur[:, ks * 128:(ks + 1) * 128], dcur[:],
                               start=True, stop=True,
                               reads=(r_pcur, r_dcur))
                        b.op("v", lambda h, pb=pb, ksg=ksg, pt=ptc:
                             h.tensor_copy(pt[:, ksg * 512:(ksg + 1) * 512],
                                           ps[pb][:]),
                             psum_reads=(r_ps[pb],), writes=(r_ptc,))
                    for ks in range(nks):
                        MM(4, ps[4][:, qt * 128:(qt + 1) * 128],
                           vc_ap(ks, kv * 128, 128),
                           ptc[:, ks * 128:(ks + 1) * 128],
                           start=(ks == 0), stop=(ks == nks - 1),
                           reads=(r_vc[0] if ks < 12 else r_vc[1], r_ptc))
                b.op("v", lambda h, hd=hd: h.tensor_copy(
                    ybuf[:, hd * 512:(hd + 1) * 512], ps[4][:]),
                    psum_reads=(r_ps[4],), writes=(r_y,))
            for dot in range(KD):
                ws = dot % 2
                b.op("g", lambda h, dot=dot, ws=ws: h.dma_start(
                    wqa[ws][:].rearrange("p (k j) -> p k j", k=KD), wo_d[dot]),
                    writes=(r_wqa[ws],), dma=f"Q{ws}")
                pb = 6 + dot % 2
                for k in range(KD):
                    MM(pb, ps[pb][:], wqa[ws][:, k * 128:(k + 1) * 128],
                       ybuf[:, k * 512:(k + 1) * 512],
                       start=(k == 0), stop=(k == KD - 1),
                       reads=(r_wqa[ws], r_y))
                xsl = xo[:, dot * TH + tc * 512:dot * TH + (tc + 1) * 512]
                b.op("v", lambda h, x=xsl, pb=pb: h.tensor_tensor(
                    x, x, ps[pb][:], ALU.add),
                    reads=(r_xo,), psum_reads=(r_ps[pb],), writes=(r_xo,))

        if dbg == "attnp":
            b.op("v", lambda h: h.tensor_copy(am[:, 0:8192], xp[:, 8192:16384]),
                 reads=(r_y,), writes=(r_am,))
            return finish(am, r_am)
        merge(r_am, r_wqa + r_pslot + r_qslot + [r_vct])
        merge(r_xp, r_ptbuf + [r_y])

        if dbg == "attn":
            return finish(xo, r_xo)
        emit_ln(xo, r_xo)
        emit_router()
        emit_experts_sparse(xo, r_xo, 2)
        return finish(xo, r_xo)



def _prep_shared(Wq, Wk, Wv, Wo, Wr, Rb, Wgu, Bgu, Wd, Bd):
    wq_h = np.ascontiguousarray(
        Wq.reshape(16, 128, 16, 128).transpose(2, 1, 0, 3))
    wo_h = np.ascontiguousarray(
        Wo.reshape(16, 128, 16, 128).transpose(2, 1, 0, 3))
    wk_h = np.ascontiguousarray(Wk.reshape(16, 128, 512).transpose(1, 0, 2))
    wv_h = np.ascontiguousarray(Wv.reshape(16, 128, 512).transpose(1, 0, 2))
    wr_h = np.ascontiguousarray(Wr.T.reshape(16, 128, E).transpose(1, 0, 2))
    rb_h = np.ascontiguousarray(Rb.reshape(1, E))
    gu = np.concatenate([Wgu[:, :, 0::2], Wgu[:, :, 1::2]], axis=2)
    wgu_h = np.ascontiguousarray(
        gu.reshape(E, 16, 128, 2, 16, 128).transpose(0, 2, 4, 3, 1, 5))
    bgu_de = np.concatenate([Bgu[:, 0::2], Bgu[:, 1::2]], axis=1)
    bgu_h = np.ascontiguousarray(bgu_de.reshape(E, 32, 128).transpose(2, 0, 1))
    wd_h = np.ascontiguousarray(
        Wd.reshape(E, 16, 128, 8, 256).transpose(0, 2, 3, 1, 4))
    bd_h = np.ascontiguousarray(Bd)
    return dict(wq=wq_h, wk=wk_h, wv=wv_h, wo=wo_h, wr=wr_h, rb=rb_h,
                wgu=wgu_h, bgu=bgu_h, wd=wd_h, bd=bd_h)


def _xt_tiles(x_half):
    return np.ascontiguousarray(
        x_half.T.reshape(16, 128, 1024).transpose(1, 0, 2))


def _sincos(h):
    inv = (1.0 / np.power(np.float32(10000.0),
                          np.arange(HALF, dtype=np.float32) /
                          np.float32(HALF))).astype(np.float32)
    par = (1 - h) * TH + np.arange(TH, dtype=np.float32)
    own = h * TH + np.arange(TH, dtype=np.float32)
    pos = np.concatenate([par, own]).astype(np.float32)
    ang = (pos[:, None] * inv[None, :]).astype(np.float32)
    s = np.sin(ang).astype(np.float32).T
    c = np.cos(ang).astype(np.float32).T
    sgn = np.concatenate([-s, s], axis=0)
    cc = np.concatenate([c, c], axis=0)
    return (np.ascontiguousarray(sgn).astype(bfd),
            np.ascontiguousarray(cc).astype(bfd))


_NC_CACHE = None


def kernel(L0_attn_x, L0_attn_Wq, L0_attn_Wk, L0_attn_Wv, L0_attn_Wo,
           L0_ffn_router_weight, L0_ffn_router_bias, L0_ffn_gate_up_proj,
           L0_ffn_gate_up_proj_bias, L0_ffn_down_proj, L0_ffn_down_proj_bias):
    global _NC_CACHE
    x = np.asarray(L0_attn_x, np.float32)
    shared = _prep_shared(
        np.asarray(L0_attn_Wq, np.float32), np.asarray(L0_attn_Wk, np.float32),
        np.asarray(L0_attn_Wv, np.float32), np.asarray(L0_attn_Wo, np.float32),
        np.asarray(L0_ffn_router_weight, np.float32),
        np.asarray(L0_ffn_router_bias, np.float32),
        np.asarray(L0_ffn_gate_up_proj, np.float32),
        np.asarray(L0_ffn_gate_up_proj_bias, np.float32),
        np.asarray(L0_ffn_down_proj, np.float32),
        np.asarray(L0_ffn_down_proj_bias, np.float32))
    in_maps = []
    for c in range(8):
        bq, h = divmod(c, 2)
        own = x[bq, h * TH:(h + 1) * TH]
        par = x[bq, (1 - h) * TH:(2 - h) * TH]
        sin_h, cos_h = _sincos(h)
        m = dict(shared)
        m["xo"] = _xt_tiles(own)
        m["xp"] = _xt_tiles(par)
        m["sin"] = sin_h
        m["cos"] = cos_h
        m["c01"] = np.full((128, 1), 0.0 if h == 1 else NEG, np.float32)
        in_maps.append(m)
    if _NC_CACHE is None:
        _NC_CACHE = build_program()
    res = run_bass_kernel_spmd(_NC_CACHE, in_maps, core_ids=list(range(8)))
    out = np.empty((4, 2048, 2048), np.float32)
    for c in range(8):
        bq, h = divmod(c, 2)
        arr = np.asarray(res.results[c]["out"])
        out[bq, h * TH:(h + 1) * TH] = arr.transpose(1, 0, 2).reshape(2048, TH).T
    return out


if __name__ == "__main__":
    nci = build_program()
    print("instructions:", nci.n_instructions())


# version 10 (final)
# speedup vs baseline: 1.0170x; 1.0170x over previous
import numpy as np
import ml_dtypes
import concourse.bass as bass
import concourse.mybir as mybir
from concourse.bass_utils import run_bass_kernel_spmd
from contextlib import ExitStack
from concourse.bass_core_rust import IndirectOffsetOnAxis

F32 = mybir.dt.float32
BF16 = mybir.dt.bfloat16
I32 = mybir.dt.int32
AX = mybir.AxisListType
ALU = mybir.AluOpType
AF = mybir.ActivationFunctionType
bfd = ml_dtypes.bfloat16

D = 2048
KD = 16
TH = 1024
E = 16
NH = 16
KVH = 4
DH = 128
HALF = 64
SCL = 1.0 / float(np.sqrt(np.float32(DH)))
NEG = -1.0e30
EPS = 1e-5
C = 384
CT = 3


class Res:
    __slots__ = ("wl", "rs", "name")
    def __init__(self, name=""):
        self.wl = []
        self.rs = []
        self.name = name


def fork(parent, children):
    for c in children:
        c.wl = list(parent.wl)
        c.rs = list(parent.rs)


def merge(parent, children):
    for c in children:
        parent.wl.extend(c.wl)
        parent.rs.extend(c.rs)


class Bld:
    ENGS = ("t", "v", "s", "g", "y")

    def __init__(self, nc, es):
        self.nc = nc
        self.ops = {e: [] for e in self.ENGS}
        self.sems = {}
        self.cnt = {}
        for name in ("t", "v", "s", "g", "I", "R0", "R1", "R2", "R3",
                     "S0", "S1", "Q0", "Q1", "KV", "OUT", "XT",
                     "G0", "G1", "G2"):
            self.sems[name] = es.enter_context(nc.semaphore("sem_" + name))
            self.cnt[name] = 0
        self.hw = {e: {} for e in self.ENGS}

    def op(self, eng, fn, reads=(), writes=(), psum_reads=(), precise=(),
           extra=(), dma=None):
        toks = []
        for r in reads:
            toks.extend(r.wl)
        for r in writes:
            toks.extend(r.wl)
            toks.extend(r.rs)
        for r in psum_reads:
            toks.extend(r.wl)
            if eng in ("v", "s"):
                other = "s" if eng == "v" else "v"
                toks.extend(t for t in r.rs if t[0] == other)
        toks.extend(extra)
        need = {}
        for se, val in toks:
            if se == eng == "t":
                continue
            if val > need.get(se, 0):
                need[se] = val
        for r in precise:
            for se, val in r.wl:
                if val > need.get(se, 0):
                    need[se] = val
        hw = self.hw[eng]
        need = {se: v for se, v in need.items() if v > hw.get(se, 0)}
        for se, v in need.items():
            hw[se] = v
        if dma:
            sname = dma
            self.cnt[sname] += 16
            tok = (sname, self.cnt[sname])
        elif fn is not None:
            self.cnt[eng] += 1
            tok = (eng, self.cnt[eng])
        else:
            tok = None
        sem = self.sems[tok[0]] if tok else None
        inc = 16 if dma else 1
        nd = sorted(need.items())
        sems = self.sems

        def thunk(h, nd=nd, fn=fn, sem=sem, inc=inc):
            for se, v in nd:
                h.wait_ge(sems[se], v)
            if fn is not None:
                ins = fn(h)
                if sem is not None:
                    ins.then_inc(sem, inc)

        self.ops[eng].append(thunk)
        if tok:
            for r in reads:
                r.rs.append(tok)
            for r in psum_reads:
                r.rs.append(tok)
            for r in precise:
                r.rs.append(tok)
            for r in writes:
                r.wl = [tok]
                r.rs = []
        return tok


_DBG = None


def build_program(dbg=None):
    nc = bass.Bass("TRN2", target_bir_lowering=False, debug=False)

    xo_d = nc.dram_tensor("xo", [128, KD, TH], F32, kind="ExternalInput").ap()
    xp_d = nc.dram_tensor("xp", [128, KD, TH], F32, kind="ExternalInput").ap()
    sin_d = nc.dram_tensor("sin", [128, 2 * TH], BF16, kind="ExternalInput").ap()
    cos_d = nc.dram_tensor("cos", [128, 2 * TH], BF16, kind="ExternalInput").ap()
    c01_d = nc.dram_tensor("c01", [128, 1], F32, kind="ExternalInput").ap()
    wq_d = nc.dram_tensor("wq", [NH, 128, KD, 128], F32, kind="ExternalInput").ap()
    wk_d = nc.dram_tensor("wk", [128, KD, 512], F32, kind="ExternalInput").ap()
    wv_d = nc.dram_tensor("wv", [128, KD, 512], F32, kind="ExternalInput").ap()
    wo_d = nc.dram_tensor("wo", [KD, 128, KD, 128], F32, kind="ExternalInput").ap()
    wr_d = nc.dram_tensor("wr", [128, KD, E], F32, kind="ExternalInput").ap()
    rb_d = nc.dram_tensor("rb", [1, E], F32, kind="ExternalInput").ap()
    wgu_d = nc.dram_tensor("wgu", [E, 128, 16, 2, 16, 128], F32,
                           kind="ExternalInput").ap()
    bgu_d = nc.dram_tensor("bgu", [128, E, 32], F32, kind="ExternalInput").ap()
    wd_d = nc.dram_tensor("wd", [E, 128, 8, 16, 256], F32,
                          kind="ExternalInput").ap()
    bd_d = nc.dram_tensor("bd", [E, D], F32, kind="ExternalInput").ap()
    out_d = nc.dram_tensor("out", [128, KD, TH], F32, kind="ExternalOutput").ap()
    kv_scr = nc.dram_tensor("kvscr", [2, 128, 8192], BF16, kind="Internal").ap()
    xtok_d = nc.dram_tensor("xtokd", [8 * 128, D], BF16, kind="Internal").ap()
    wcache_a = nc.dram_tensor("wcachea", [E // 2, 24, 128, 4096], BF16,
                              kind="Internal").ap()
    wcache_b = nc.dram_tensor("wcacheb", [E // 2, 24, 128, 4096], BF16,
                              kind="Internal").ap()
    wcache = lambda e, u: (wcache_a if e < E // 2 else wcache_b)[e % (E // 2), u]

    ident_c = nc.inline_tensor(np.eye(128, dtype=bfd), "identc").ap()
    ones128_c = nc.inline_tensor(np.ones((128, 128), dtype=bfd), "ones128c").ap()
    ones1_c = nc.inline_tensor(np.ones((1, 128), dtype=bfd), "ones1c").ap()
    sel_np = np.zeros((E, E * 128), dtype=bfd)
    for e in range(E):
        sel_np[e, e * 128:(e + 1) * 128] = 1.0
    sel_c = nc.inline_tensor(sel_np, "selc").ap()
    tri_np = np.zeros((128, 4 * 512), np.float32)
    for s in range(4):
        for p in range(128):
            tri_np[p, s * 512:(s + 1) * 512][np.arange(512) > s * 128 + p] = NEG
    tri_c = nc.inline_tensor(tri_np.astype(bfd), "tric").ap()
    eps_c = nc.inline_tensor(np.full((128, 1), EPS, np.float32), "epsc").ap()
    tid = np.arange(1024).reshape(8, 128).T
    ioth_c = nc.inline_tensor((tid // 4).astype(bfd), "iothc").ap()
    iotl_c = nc.inline_tensor((tid % 4).astype(bfd), "iotlc").ap()
    iot_np = np.tile(np.arange(C, dtype=np.float32)[None, :], (128, 1))
    iot_c = nc.inline_tensor(iot_np, "iotc").ap()
    ult_np = np.zeros((128, 128), dtype=bfd)
    for k in range(128):
        ult_np[k, k + 1:] = 1.0
    ult_c = nc.inline_tensor(ult_np, "ultc").ap()
    swap_np = np.zeros((128, 128), dtype=bfd)
    for k in range(128):
        swap_np[k, (k + 64) % 128] = 1.0
    swap_c = nc.inline_tensor(swap_np, "swapc").ap()

    es = ExitStack()
    with es:
        sb = lambda name, shape, dt: es.enter_context(
            nc.sbuf_tensor(name, shape, dt))
        xo = sb("xo_s", [128, KD * TH], BF16)
        xp = sb("xp_s", [128, KD * TH], BF16)
        xn = sb("xn_s", [128, KD * TH], BF16)
        am = sb("am_s", [128, KD * TH], BF16)
        rg = [sb(f"rg{i}", [128, 4096], BF16) for i in range(2)]
        xg = sb("xg_s", [128, KD * C], BF16)
        scr = sb("scr_s", [128, 3584], F32)
        scb = sb("scb_s", [128, 2048], BF16)
        ident = sb("ident_s", [128, 128], BF16)
        swapb = sb("swap_s", [128, 128], BF16)
        iotb = sb("iot_s", [128, C], F32)
        iothb = sb("ioth_s", [128, 8], BF16)
        iotlb = sb("iotl_s", [128, 8], BF16)
        idxf = sb("idxf_s", [128, 2], F32)
        idxi = sb("idxi_s", [128, CT * 2], I32)
        ultb = sb("ult_s", [128, 128], BF16)
        Mb_sb = sb("Mb_s", [128, 128], BF16)
        rankm = sb("rankm_s", [128, 128], F32)
        Shtok = sb("Shtok_s", [128, 128], F32)
        ones128 = sb("ones128_s", [128, 128], BF16)
        ones1 = sb("ones1_s", [1, 128], BF16)
        sel = sb("sel_s", [E, E * 128], BF16)
        tri = sb("tri_s", [128, 4 * 512], BF16)
        epsc = sb("eps_s", [128, 1], F32)
        sinb = sb("sin_s", [128, 2 * TH], BF16)
        cosb = sb("cos_s", [128, 2 * TH], BF16)
        c01 = sb("c01_s", [128, 1], F32)
        wr_sb = sb("wr_sb", [128, KD * E], BF16)
        rb_sb = sb("rb_sb", [1, E], BF16)
        bgu_sb = sb("bgu_sb", [128, E * 32], F32)
        bd_sb = sb("bd_sb", [E, D], BF16)
        lg = sb("lg_s", [128, E], F32)
        m4 = sb("m4_s", [128, 4], F32)
        nm0 = sb("nm0_s", [128, 1], F32)
        e4 = sb("e4_s", [128, 4], F32)
        s4 = sb("s4_s", [128, 1], F32)
        rs4 = sb("rs4_s", [128, 1], F32)
        w4 = sb("w4_s", [128, 4], F32)
        msk4 = sb("msk4_s", [128, 4 * E], F32)
        Sll = sb("S_s", [128, E], F32)
        Shbf = sb("Shbf_s", [128, E], BF16)
        ShT = sb("ShT_s", [E, TH], BF16)
        sums4 = sb("sums4_s", [128, 4], F32)
        sumt = sb("sumt_s", [128, 1], F32)
        rst = sb("rst_s", [128, 1], F32)
        Dt = [sb(f"Dt{i}", [128, 128], BF16) for i in range(2)]
        k2m = sb("k2m_s", [1, KVH], F32)
        k2c = sb("k2c_s", [1, 1], F32)
        q2m = sb("q2m_s", [1, 1], F32)
        b2 = sb("b2_s", [1, 1], F32)
        lnb2 = sb("lnb2_s", [1, 1], F32)
        bsq = sb("bsq_s", [1, 1], F32)
        nbv = sb("nbv_s", [1, 1], F32)
        nbb = sb("nbb_s", [1, 1], BF16)
        bc = sb("bc_s", [128, 1], F32)
        bc01 = sb("bc01_s", [128, 1], F32)

        ps = [es.enter_context(nc.psum_tensor(f"ps{i}", [128, 512], F32))
              for i in range(8)]

        b = Bld(nc, es)

        r_xo, r_xp, r_xn, r_am = Res("xo"), Res("xp"), Res("xn"), Res("am")
        r_rg = [Res(f"rg{i}") for i in range(2)]
        r_scr, r_scb = Res("scr"), Res("scb")
        r_ps = [Res(f"ps{i}") for i in range(8)]
        r_consts = Res("consts")
        r_xg = Res("xg")
        r_wc = Res("wcache")
        r_Mb, r_rankm, r_Shtok = Res("Mb"), Res("rankm"), Res("Shtok")
        r_idxf, r_idxi, r_xtokd = Res("idxf"), Res("idxi"), Res("xtokd")
        r_small = Res("smallw")
        r_lg, r_m4, r_nm0, r_e4 = Res("lg"), Res("m4"), Res("nm0"), Res("e4")
        r_s4, r_rs4, r_w4, r_msk4 = Res("s4"), Res("rs4"), Res("w4"), Res("msk4")
        r_S, r_Shbf, r_ShT = Res("S"), Res("Shbf"), Res("ShT")
        r_sums4, r_sumt, r_rst = Res("sums4"), Res("sumt"), Res("rst")
        r_Dt = [Res("Dt0"), Res("Dt1")]
        r_k2m, r_k2c, r_q2m, r_b2 = Res("k2m"), Res("k2c"), Res("q2m"), Res("b2")
        r_lnb2, r_bsq, r_nbv, r_nbb = Res("lnb2"), Res("bsq"), Res("nbv"), Res("nbb")
        r_bc, r_bc01 = Res("bc"), Res("bc01")
        r_kvscr = Res("kvscr")
        r_sc = Res("sincos")

        for dst, src, rr in ((ident, ident_c, r_consts),
                             (ones128, ones128_c, r_consts),
                             (ones1, ones1_c, r_consts),
                             (sel, sel_c, r_consts),
                             (tri, tri_c, r_consts),
                             (epsc, eps_c, r_consts),
                             (sinb, sin_d, r_sc),
                             (cosb, cos_d, r_sc),
                             (swapb, swap_c, r_consts),
                             (iotb, iot_c, r_consts),
                             (iothb, ioth_c, r_consts),
                             (iotlb, iotl_c, r_consts),
                             (ultb, ult_c, r_consts),
                             (c01, c01_d, r_consts)):
            b.op("y", lambda h, dst=dst, src=src: h.dma_start(dst[:], src),
                 writes=(rr,), dma="I")
        b.op("y", lambda h: h.dma_start(
            bgu_sb[:].rearrange("p (e s) -> p e s", e=E), bgu_d),
            writes=(r_small,), dma="I")
        b.op("g", lambda h: h.dma_start(
            wr_sb[:].rearrange("p (k e) -> p k e", k=KD), wr_d),
            writes=(r_small,), dma="I")
        b.op("g", lambda h: h.dma_start(rb_sb[:], rb_d), writes=(r_small,),
             dma="I")
        b.op("g", lambda h: h.dma_start(bd_sb[:], bd_d), writes=(r_small,),
             dma="I")
        b.op("g", lambda h: h.dma_start(
            xp[:].rearrange("p (k t) -> p k t", k=KD), xp_d),
            writes=(r_xp,), dma="I")
        b.op("g", lambda h: h.dma_start(
            xo[:].rearrange("p (k t) -> p k t", k=KD), xo_d),
            writes=(r_xo,), dma="I")
        for rr in (r_consts, r_sc, r_small, r_xp, r_xo):
            rr.wl = [("I", b.cnt["I"])]

        def MM(pbank, out_ap, lhsT_ap, rhs_ap, start, stop, reads=()):
            return b.op("t", lambda h, o=out_ap, l=lhsT_ap, r=rhs_ap,
                        st=start, sp=stop: h.matmul(o, l, r, start=st, stop=sp),
                        reads=reads, writes=(r_ps[pbank],))

        def emit_ln(xbuf, r_xbuf):
            mean = scr[:, 0:512]
            rstd = scr[:, 512:1024]
            tmp = scr[:, 1024:1536]
            for tc in range(2):
                for k in range(KD):
                    pp = k % 2
                    xsl = xbuf[:, k * TH + tc * 512:k * TH + (tc + 1) * 512]
                    xb_ap = scb[:, pp * 512:(pp + 1) * 512]
                    xq_ap = scb[:, 1024 + pp * 512:1024 + (pp + 1) * 512]
                    b.op("v", lambda h, a=xb_ap, x=xsl: h.tensor_copy(a, x),
                         reads=(r_xbuf,), writes=(r_scb,))
                    b.op("s", lambda h, a=xq_ap, x=xb_ap: h.activation(
                        a, x, AF.Square), reads=(r_scb,), writes=(r_scb,))
                    MM(6, ps[6][:], ones128[:], xb_ap,
                       start=(k == 0), stop=(k == KD - 1),
                       reads=(r_scb, r_consts))
                    MM(7, ps[7][:], ones128[:], xq_ap,
                       start=(k == 0), stop=(k == KD - 1),
                       reads=(r_scb, r_consts))
                b.op("v", lambda h: h.tensor_scalar_mul(mean, ps[6][:], 1.0 / D),
                     psum_reads=(r_ps[6],), writes=(r_scr,))
                b.op("v", lambda h: h.tensor_mul(tmp, mean, mean),
                     reads=(r_scr,), writes=(r_scr,))
                b.op("v", lambda h: h.tensor_scalar_mul(rstd, ps[7][:], 1.0 / D),
                     psum_reads=(r_ps[7],), writes=(r_scr,))
                b.op("v", lambda h: h.tensor_sub(rstd, rstd, tmp),
                     reads=(r_scr,), writes=(r_scr,))
                b.op("s", lambda h: h.activation(rstd, rstd, AF.Ln, bias=epsc[:]),
                     reads=(r_scr, r_consts), writes=(r_scr,))
                b.op("s", lambda h: h.activation(rstd, rstd, AF.Exp, scale=-0.5),
                     reads=(r_scr,), writes=(r_scr,))
                for k in range(KD):
                    xsl = xbuf[:, k * TH + tc * 512:k * TH + (tc + 1) * 512]
                    nsl = xn[:, k * TH + tc * 512:k * TH + (tc + 1) * 512]
                    b.op("v", lambda h, x=xsl: h.tensor_sub(tmp, x, mean),
                         reads=(r_xbuf, r_scr), writes=(r_scr,))
                    b.op("v", lambda h, n=nsl: h.tensor_mul(n, tmp, rstd),
                         reads=(r_scr,), writes=(r_xn,))

        def emit_router(limit="full"):
            for tt in range(8):
                pb = 6 + (tt % 2)
                lga = ps[pb][:, 0:E]
                for k in range(KD):
                    MM(pb, lga,
                       xn[:, k * TH + tt * 128:k * TH + (tt + 1) * 128],
                       wr_sb[:, k * E:(k + 1) * E],
                       start=(k == 0), stop=False, reads=(r_xn, r_small))
                MM(pb, lga, ones1[:], rb_sb[:], start=False, stop=True,
                   reads=(r_consts, r_small))
                b.op("v", lambda h, a=lga: h.tensor_copy(lg[:], a),
                     psum_reads=(r_ps[pb],), writes=(r_lg,))
                if limit == "lg":
                    continue
                for j in range(4):
                    b.op("v", lambda h, j=j: h.reduce_max(
                        m4[:, j:j + 1], lg[:], axis=AX.X),
                        reads=(r_lg,), writes=(r_m4,))
                    b.op("v", lambda h, j=j: h.tensor_scalar(
                        msk4[:, j * E:(j + 1) * E], lg[:], m4[:, j:j + 1],
                        None, ALU.is_ge),
                        reads=(r_lg,), precise=(r_m4,), writes=(r_msk4,))
                    if j < 3:
                        b.op("v", lambda h, j=j: h.scalar_tensor_tensor(
                            lg[:], msk4[:, j * E:(j + 1) * E], NEG, lg[:],
                            ALU.mult, ALU.add),
                            reads=(r_msk4, r_lg), writes=(r_lg,))
                b.op("v", lambda h: h.tensor_scalar_mul(nm0[:], m4[:, 0:1], -1.0),
                     reads=(r_m4,), writes=(r_nm0,))
                b.op("s", lambda h: h.activation(e4[:], m4[:], AF.Exp,
                                                 bias=nm0[:]),
                     reads=(r_m4, r_nm0), writes=(r_e4,))
                b.op("v", lambda h: h.reduce_sum(s4[:], e4[:], axis=AX.X),
                     reads=(r_e4,), writes=(r_s4,))
                b.op("v", lambda h: h.reciprocal(rs4[:], s4[:]),
                     reads=(r_s4,), writes=(r_rs4,))
                b.op("v", lambda h: h.tensor_scalar(
                    w4[:], e4[:], rs4[:, 0:1], None, ALU.mult),
                    reads=(r_e4,), precise=(r_rs4,), writes=(r_w4,))
                if limit == "top4":
                    continue
                b.op("v", lambda h: h.tensor_scalar(
                    Sll[:], msk4[:, 0:E], w4[:, 0:1], None, ALU.mult),
                    reads=(r_msk4,), precise=(r_w4,), writes=(r_S,))
                for j in range(1, 4):
                    b.op("v", lambda h, j=j: h.scalar_tensor_tensor(
                        Sll[:], msk4[:, j * E:(j + 1) * E], w4[:, j:j + 1],
                        Sll[:], ALU.mult, ALU.add),
                        reads=(r_msk4, r_S), precise=(r_w4,), writes=(r_S,))
                if limit == "sacc":
                    continue
                b.op("v", lambda h: h.tensor_scalar_mul(Shbf[:], Sll[:], 0.5),
                     reads=(r_S,), writes=(r_Shbf,))
                if limit == "shbf":
                    continue
                MM(pb, ps[pb][0:E, 0:128], Shbf[:], ident[:],
                   start=True, stop=True, reads=(r_Shbf, r_consts))
                if limit == "tpose":
                    continue
                b.op("v", lambda h, tt=tt, pb=pb: h.tensor_copy(
                    ShT[:, tt * 128:(tt + 1) * 128], ps[pb][0:E, 0:128]),
                    psum_reads=(r_ps[pb],), writes=(r_ShT,))
                if limit == "sht":
                    continue
                b.op("v", lambda h: h.tensor_add(lg[:], msk4[:, 0:E],
                                                 msk4[:, E:2 * E]),
                     reads=(r_msk4,), writes=(r_lg,))
                b.op("v", lambda h: h.tensor_add(lg[:], lg[:],
                                                 msk4[:, 2 * E:3 * E]),
                     reads=(r_msk4, r_lg), writes=(r_lg,))
                b.op("v", lambda h: h.tensor_add(lg[:], lg[:],
                                                 msk4[:, 3 * E:4 * E]),
                     reads=(r_msk4, r_lg), writes=(r_lg,))
                b.op("v", lambda h, tt=tt: h.tensor_copy(
                    Mb_sb[:, tt * E:(tt + 1) * E], lg[:]),
                    reads=(r_lg,), writes=(r_Mb,))
                b.op("v", lambda h, tt=tt: h.tensor_scalar_mul(
                    Shtok[:, tt * E:(tt + 1) * E], Sll[:], 0.5),
                    reads=(r_S,), writes=(r_Shtok,))
            if limit != "full":
                return
            for tt in range(8):
                pb = 6 + (tt % 2)
                rka = ps[pb][:, 0:E]
                first = True
                for t2 in range(tt):
                    MM(pb, rka, ones128[:], Mb_sb[:, t2 * E:(t2 + 1) * E],
                       start=first, stop=False, reads=(r_consts, r_Mb))
                    first = False
                MM(pb, rka, ultb[:], Mb_sb[:, tt * E:(tt + 1) * E],
                   start=first, stop=True, reads=(r_consts, r_Mb))
                b.op("v", lambda h, rka=rka: h.tensor_scalar_add(
                    lg[:], rka, 1.0), psum_reads=(r_ps[pb],), writes=(r_lg,))
                b.op("v", lambda h, tt=tt: h.tensor_tensor(
                    lg[:], lg[:], Mb_sb[:, tt * E:(tt + 1) * E], ALU.mult),
                    reads=(r_lg, r_Mb), writes=(r_lg,))
                b.op("v", lambda h, tt=tt: h.tensor_scalar(
                    rankm[:, tt * E:(tt + 1) * E], lg[:], -1.0, None, ALU.add),
                    reads=(r_lg,), writes=(r_rankm,))

        def emit_experts_sparse(xbuf, r_xbuf, sweep):
            r_act, r_Gc, r_gtb, r_gts = (Res("act"), Res("Gc"), Res("gtb"),
                                         Res("gts"))
            fork(r_xn, [r_act, r_Gc, r_gtb, r_gts])
            r_stg = Res("stg")
            r_gslot = [Res("g0"), Res("g1"), Res("g2")]
            r_rge = [Res("rge0"), Res("rge1")]
            fork(r_am, [r_stg] + r_gslot + r_rge)
            act = xn
            Gc = xn
            gtb = xn
            gts = [xn[:, 6528:6528 + C], xn[:, 6912:6912 + C]]
            rgx = [rg[0][:], rg[1][:], am[:, 8192:12288], am[:, 12288:16384]]
            r_rgx = [r_rg[0], r_rg[1], r_rge[0], r_rge[1]]
            gsl_ap = [am[:, 2048 + i * 2048:2048 + (i + 1) * 2048]
                      for i in range(3)]

            for tt in range(8):
                pb2 = 2 + (tt % 2)
                so = (tt % 2) * 2048
                r_so = r_stg if tt % 2 == 0 else r_gslot[0]
                for dsl in range(KD):
                    MM(pb2, ps[pb2][:, (dsl % 4) * 128:(dsl % 4) * 128 + 128],
                       xn[:, dsl * TH + tt * 128:dsl * TH + (tt + 1) * 128],
                       ident[:], start=True, stop=True,
                       reads=(r_xn, r_consts))
                    if dsl % 4 == 3:
                        b.op("v", lambda h, dsl=dsl, pb2=pb2, so=so:
                             h.tensor_copy(
                                 am[:, so + (dsl - 3) * 128:
                                    so + (dsl + 1) * 128],
                                 ps[pb2][:]),
                             psum_reads=(r_ps[pb2],), writes=(r_so,))
                b.op("y", lambda h, tt=tt, so=so: h.dma_start(
                    xtok_d[tt * 128:(tt + 1) * 128, :],
                    am[:, so:so + 2048]),
                    reads=(r_so,), writes=(r_xtokd,), dma="XT")
            r_xtokd.wl = [("XT", b.cnt["XT"])]

            def build_gt(e):
                for tt in range(8):
                    rcol = rankm[:, tt * E + e:tt * E + e + 1]
                    b.op("v", lambda h, tt=tt, rcol=rcol: h.tensor_scalar(
                        gtb[:, 12288 + tt * C:12288 + (tt + 1) * C],
                        iotb[:], rcol, None, ALU.is_equal),
                        reads=(r_consts,), precise=(r_rankm,), writes=(r_gtb,))

            def build_idx(e, par):
                for ct in range(CT):
                    for tt in range(8):
                        MM(6, ps[6][:, 0:1],
                           gtb[:, 12288 + tt * C + ct * 128:
                               12288 + tt * C + (ct + 1) * 128],
                           iothb[:, tt:tt + 1], start=(tt == 0), stop=False,
                           reads=(r_gtb, r_consts))
                    for tt in range(8):
                        MM(6, ps[6][:, 1:2],
                           gtb[:, 12288 + tt * C + ct * 128:
                               12288 + tt * C + (ct + 1) * 128],
                           iotlb[:, tt:tt + 1], start=(tt == 0),
                           stop=(tt == 7), reads=(r_gtb, r_consts))
                    b.op("v", lambda h: h.tensor_copy(
                        idxf[:, 1:2], ps[6][:, 0:1]),
                        psum_reads=(r_ps[6],), writes=(r_idxf,))
                    b.op("v", lambda h: h.scalar_tensor_tensor(
                        idxf[:, 0:1], idxf[:, 1:2], 4.0, ps[6][:, 1:2],
                        ALU.mult, ALU.add),
                        reads=(r_idxf,), psum_reads=(r_ps[6],),
                        writes=(r_idxf,))
                    b.op("v", lambda h, ct=ct, par=par: h.tensor_copy(
                        idxi[:, par * CT + ct:par * CT + ct + 1],
                        idxf[:, 0:1]),
                        reads=(r_idxf,), writes=(r_idxi,))

            def build_gc(e):
                for ttg in range(2):
                    for tt4 in range(4):
                        tt = ttg * 4 + tt4
                        gsl = gts[tt % 2]
                        rcol = rankm[:, tt * E + e:tt * E + e + 1]
                        scol = Shtok[:, tt * E + e:tt * E + e + 1]
                        b.op("v", lambda h, gsl=gsl, rcol=rcol, scol=scol:
                             h.tensor_scalar(gsl, iotb[:], rcol, scol,
                                             ALU.is_equal, ALU.mult),
                             reads=(r_consts,), precise=(r_rankm, r_Shtok),
                             writes=(r_gts,))
                        for ct in range(CT):
                            MM(2 + ct, ps[2 + ct][:, tt4 * 128:(tt4 + 1) * 128],
                               gsl[:, ct * 128:(ct + 1) * 128], ident[:],
                               start=True, stop=True,
                               reads=(r_gts, r_consts))
                    for ct in range(CT):
                        b.op("v", lambda h, ct=ct, ttg=ttg: h.tensor_copy(
                            Gc[:, 9216 + ct * 1024 + ttg * 512:
                               9216 + ct * 1024 + (ttg + 1) * 512],
                            ps[2 + ct][:]),
                            psum_reads=(r_ps[2 + ct],), writes=(r_Gc,))

            gs = scr[:, 1536:1536 + C]
            sg = scr[:, 2048:2048 + C]
            gl = scr[:, 2560:2560 + C]
            u2 = scr[:, 3072:3072 + C]
            def emit_gathers(e):
                par = e % 2
                for ct in range(CT):
                    b.op("g", lambda h, ct=ct, par=par: h.indirect_dma_start(
                        gsl_ap[ct], None, xtok_d,
                        IndirectOffsetOnAxis(
                            idxi[:, par * CT + ct:par * CT + ct + 1], 0)),
                        reads=(r_idxi, r_xtokd), writes=(r_gslot[ct],),
                        dma=f"G{ct}")

            build_gt(0)
            build_idx(0, 0)
            emit_gathers(0)
            for e in range(E):
                par = e % 2
                for dsl in range(KD):
                    pb2 = dsl % 4
                    for ct in range(CT):
                        MM(pb2, ps[pb2][:, ct * 128:(ct + 1) * 128],
                           gsl_ap[ct][:, dsl * 128:(dsl + 1) * 128],
                           ident[:], start=True, stop=True,
                           reads=(r_gslot[ct], r_consts))
                    b.op("v", lambda h, dsl=dsl, pb2=pb2: h.tensor_copy(
                        xg[:, dsl * C:(dsl + 1) * C], ps[pb2][:, 0:C]),
                        psum_reads=(r_ps[pb2],), writes=(r_xg,))
                if e + 1 < E:
                    build_gt(e + 1)
                    build_idx(e + 1, 1 - par)
                build_gc(e)
                if e + 1 < E:
                    build_gt(e + 1)
                for jc in range(16):
                    slot = (e * 24 + jc) % 4
                    if sweep == 0:
                        b.op("g", lambda h, jc=jc, e=e, slot=slot: h.dma_start(
                            rgx[slot].rearrange("p (g k j) -> p g k j",
                                                g=2, k=16),
                            wgu_d[e, :, jc]), writes=(r_rgx[slot],),
                            dma=f"R{slot}")
                        b.op("y", lambda h, e=e, jc=jc, slot=slot: h.dma_start(
                            wcache(e, jc), rgx[slot]),
                            reads=(r_rgx[slot],), writes=(r_wc,),
                            dma=f"S{slot % 2}")
                    else:
                        b.op("y", lambda h, e=e, jc=jc, slot=slot: h.dma_start(
                            rgx[slot], wcache(e, jc)),
                            reads=(r_wc,), writes=(r_rgx[slot],),
                            dma=f"R{slot}")
                    da = jc
                    pg = jc % 2
                    pu = 2 + jc % 2
                    for k in range(KD):
                        MM(pg, ps[pg][:, 0:C],
                           rgx[slot][:, k * 128:(k + 1) * 128],
                           xg[:, k * C:(k + 1) * C],
                           start=(k == 0), stop=(k == KD - 1),
                           reads=(r_rgx[slot], r_xg))
                    for k in range(KD):
                        MM(pu, ps[pu][:, 0:C],
                           rgx[slot][:, 2048 + k * 128:2048 + (k + 1) * 128],
                           xg[:, k * C:(k + 1) * C],
                           start=(k == 0), stop=(k == KD - 1),
                           reads=(r_rgx[slot], r_xg))
                    gcol = bgu_sb[:, e * 32 + da:e * 32 + da + 1]
                    ucol = bgu_sb[:, e * 32 + 16 + da:e * 32 + 16 + da + 1]
                    b.op("v", lambda h, pg=pg, gc=gcol: h.tensor_scalar(
                        gs, ps[pg][:, 0:C], gc, 7.0, ALU.add, ALU.min),
                        psum_reads=(r_ps[pg],), reads=(r_small,),
                        writes=(r_scr,))
                    b.op("s", lambda h: h.activation(
                        sg, gs, AF.Sigmoid, scale=1.702),
                        reads=(r_scr,), writes=(r_scr,))
                    b.op("v", lambda h: h.tensor_mul(gl, gs, sg),
                         reads=(r_scr,), writes=(r_scr,))
                    b.op("v", lambda h, pu=pu, uc=ucol: h.tensor_scalar(
                        u2, ps[pu][:, 0:C], uc, 7.0, ALU.add, ALU.min),
                        psum_reads=(r_ps[pu],), reads=(r_small,),
                        writes=(r_scr,))
                    b.op("v", lambda h: h.tensor_scalar(
                        u2, u2, -7.0, 1.0, ALU.max, ALU.add),
                        reads=(r_scr,), writes=(r_scr,))
                    b.op("v", lambda h, da=da: h.tensor_mul(
                        act[:, da * C:(da + 1) * C], u2, gl),
                        reads=(r_scr,), writes=(r_act,))
                last = (e == E - 1)
                for hc in range(4):
                    for hh in range(2):
                        slot = (e * 24 + 16 + hc * 2 + hh) % 4
                        if sweep == 0:
                            b.op("g", lambda h, e=e, u=hc * 2 + hh, slot=slot:
                                 h.dma_start(
                                     rgx[slot].rearrange("p (k j) -> p k j",
                                                         k=16),
                                     wd_d[e, :, u]), writes=(r_rgx[slot],),
                                 dma=f"R{slot}")
                            b.op("y", lambda h, e=e, u=16 + hc * 2 + hh,
                                 slot=slot: h.dma_start(
                                     wcache(e, u), rgx[slot]),
                                 reads=(r_rgx[slot],), writes=(r_wc,),
                                 dma=f"S{slot % 2}")
                        else:
                            b.op("y", lambda h, e=e, u=16 + hc * 2 + hh,
                                 slot=slot: h.dma_start(
                                     rgx[slot], wcache(e, u)),
                                 reads=(r_wc,), writes=(r_rgx[slot],),
                                 dma=f"R{slot}")
                        for ct in range(CT):
                            pd = 4 + (ct + hh) % 2
                            for k in range(KD):
                                MM(pd, ps[pd][:, 0:256],
                                   act[:, k * C + ct * 128:
                                       k * C + (ct + 1) * 128],
                                   rgx[slot][:, k * 256:(k + 1) * 256],
                                   start=(k == 0), stop=(k == KD - 1),
                                   reads=(r_act, r_rgx[slot]))
                            b.op("v", lambda h, ct=ct, hh=hh, pd=pd:
                                 h.tensor_copy(
                                     scb[:, ct * 512 + hh * 256:
                                         ct * 512 + (hh + 1) * 256],
                                     ps[pd][:, 0:256]),
                                 psum_reads=(r_ps[pd],), writes=(r_scb,))
                    for ht4 in range(4):
                        ht = hc * 4 + ht4
                        for tc in range(2):
                            pd = 6 + (ht4 * 2 + tc) % 2
                            for ct in range(CT):
                                MM(pd, ps[pd][:],
                                   scb[:, ct * 512 + ht4 * 128:
                                       ct * 512 + (ht4 + 1) * 128],
                                   Gc[:, 9216 + ct * 1024 + tc * 512:
                                      9216 + ct * 1024 + (tc + 1) * 512],
                                   start=(ct == 0),
                                   stop=(ct == CT - 1 and not last),
                                   reads=(r_scb, r_Gc))
                            if last:
                                MM(pd, ps[pd][:],
                                   bd_sb[0:E, ht * 128:(ht + 1) * 128],
                                   ShT[:, tc * 512:(tc + 1) * 512],
                                   start=False, stop=True,
                                   reads=(r_small, r_ShT))
                            xsl = xbuf[:, ht * TH + tc * 512:
                                       ht * TH + (tc + 1) * 512]
                            b.op("v", lambda h, x=xsl, pd=pd: h.tensor_tensor(
                                x, x, ps[pd][:], ALU.add),
                                reads=(r_xbuf,), psum_reads=(r_ps[pd],),
                                writes=(r_xbuf,))
                if e + 1 < E:
                    emit_gathers(e + 1)
            merge(r_xn, [r_act, r_Gc, r_gtb, r_gts])
            merge(r_am, [r_stg] + r_gslot + r_rge)

        def emit_experts(xbuf, r_xbuf):
            gs = scr[:, 1536:2048]
            sg = scr[:, 2048:2560]
            gl = scr[:, 2560:3072]
            u2 = scr[:, 3072:3584]
            for e in range(E):
                for tc in range(2):
                    MM(4 + tc, ps[4 + tc][:], sel[:, e * 128:(e + 1) * 128],
                       ShT[:, tc * 512:(tc + 1) * 512], start=True, stop=True,
                       reads=(r_consts, r_ShT))
                for jc in range(8):
                    slot = (e * 12 + jc) % 2
                    b.op("g", lambda h, e=e, jc=jc, slot=slot: h.dma_start(
                        rg[slot][:].rearrange("p (g k j) -> p g k j", g=2, k=16),
                        wgu_d[e, :, jc]), writes=(r_rg[slot],), dma="W")
                    for js in range(2):
                        da = jc * 2 + js
                        for tc in range(2):
                            pg = (da * 2 + tc) % 2
                            pu = 2 + (da * 2 + tc) % 2
                            xnsl = [xn[:, k * TH + tc * 512:
                                       k * TH + (tc + 1) * 512]
                                    for k in range(KD)]
                            for k in range(KD):
                                wsl = rg[slot][:, k * 256 + js * 128:
                                               k * 256 + js * 128 + 128]
                                MM(pg, ps[pg][:], wsl, xnsl[k],
                                   start=(k == 0), stop=(k == KD - 1),
                                   reads=(r_rg[slot], r_xn))
                            for k in range(KD):
                                wsl = rg[slot][:, 4096 + k * 256 + js * 128:
                                               4096 + k * 256 + js * 128 + 128]
                                MM(pu, ps[pu][:], wsl, xnsl[k],
                                   start=(k == 0), stop=(k == KD - 1),
                                   reads=(r_rg[slot], r_xn))
                            gcol = bgu_sb[:, e * 32 + da:e * 32 + da + 1]
                            ucol = bgu_sb[:, e * 32 + 16 + da:
                                          e * 32 + 16 + da + 1]
                            b.op("v", lambda h, pg=pg, gc=gcol: h.tensor_scalar(
                                gs, ps[pg][:], gc, 7.0, ALU.add, ALU.min),
                                psum_reads=(r_ps[pg],), reads=(r_small,),
                                writes=(r_scr,))
                            b.op("s", lambda h: h.activation(
                                sg, gs, AF.Sigmoid, scale=1.702),
                                reads=(r_scr,), writes=(r_scr,))
                            b.op("v", lambda h: h.tensor_mul(gl, gs, sg),
                                 reads=(r_scr,), writes=(r_scr,))
                            b.op("v", lambda h, pu=pu, uc=ucol: h.tensor_scalar(
                                u2, ps[pu][:], uc, 7.0, ALU.add, ALU.min),
                                psum_reads=(r_ps[pu],), reads=(r_small,),
                                writes=(r_scr,))
                            b.op("v", lambda h: h.tensor_scalar(
                                u2, u2, -7.0, 1.0, ALU.max, ALU.add),
                                reads=(r_scr,), writes=(r_scr,))
                            b.op("v", lambda h, tc=tc: h.tensor_tensor(
                                gl, gl, ps[4 + tc][:], ALU.mult),
                                reads=(r_scr,), psum_reads=(r_ps[4 + tc],),
                                writes=(r_scr,))
                            asl = am[:, da * TH + tc * 512:
                                     da * TH + (tc + 1) * 512]
                            b.op("v", lambda h, a=asl: h.tensor_mul(a, u2, gl),
                                 reads=(r_scr,), writes=(r_am,))
                last = (e == E - 1)
                for hc in range(4):
                    slot = (e * 12 + 8 + hc) % 2
                    b.op("g", lambda h, e=e, hc=hc, slot=slot: h.dma_start(
                        rg[slot][:].rearrange("p (k j) -> p k j", k=16),
                        wd_d[e, :, hc]), writes=(r_rg[slot],), dma="W")
                    for hs in range(4):
                        ht = hc * 4 + hs
                        for tc in range(2):
                            pd = 6 + (hs * 2 + tc) % 2
                            for k in range(KD):
                                wsl = rg[slot][:, k * 512 + hs * 128:
                                               k * 512 + hs * 128 + 128]
                                MM(pd, ps[pd][:], wsl,
                                   am[:, k * TH + tc * 512:
                                      k * TH + (tc + 1) * 512],
                                   start=(k == 0),
                                   stop=(k == KD - 1 and not last),
                                   reads=(r_rg[slot], r_am))
                            if last:
                                MM(pd, ps[pd][:],
                                   bd_sb[0:E, ht * 128:(ht + 1) * 128],
                                   ShT[:, tc * 512:(tc + 1) * 512],
                                   start=False, stop=True,
                                   reads=(r_small, r_ShT))
                            xsl = xbuf[:, ht * TH + tc * 512:
                                       ht * TH + (tc + 1) * 512]
                            b.op("v", lambda h, x=xsl, pd=pd: h.tensor_tensor(
                                x, x, ps[pd][:], ALU.add),
                                reads=(r_xbuf,), psum_reads=(r_ps[pd],),
                                writes=(r_xbuf,))

        def emit_rope(pb, pb2, dst, t0, r_dst):
            qev = scb[:, 0:512]
            m1 = scr[:, 1536:2048]
            m2 = scr[:, 2048:2560]
            csl = cosb[:, t0:t0 + 512]
            ssl = sinb[:, t0:t0 + 512]
            b.op("v", lambda h: h.tensor_copy(qev, ps[pb][:]),
                 psum_reads=(r_ps[pb],), writes=(r_scb,))
            MM(pb2, ps[pb2][:], swapb[:], qev, start=True, stop=True,
               reads=(r_scb, r_consts))
            b.op("v", lambda h, c=csl: h.tensor_tensor(m1, qev, c, ALU.mult),
                 reads=(r_scb, r_sc), writes=(r_scr,))
            b.op("v", lambda h, s=ssl, pb2=pb2: h.tensor_tensor(
                m2, ps[pb2][:], s, ALU.mult),
                psum_reads=(r_ps[pb2],), reads=(r_sc,), writes=(r_scr,))
            b.op("v", lambda h, d=dst: h.tensor_add(d, m1, m2),
                 reads=(r_scr,), writes=(r_dst,))

        def emit_kv(half, pos0):
            kvst = am[:, 0:8192]
            r_kvst = Res("kvst")
            fork(r_am, [r_kvst])
            b.op("g", lambda h: h.dma_start(
                rg[0][:].rearrange("p (k j) -> p k j", k=8), wk_d[:, 0:8]),
                writes=(r_rg[0],), dma="R0")
            b.op("g", lambda h: h.dma_start(
                rg[1][:].rearrange("p (k j) -> p k j", k=8), wk_d[:, 8:16]),
                writes=(r_rg[1],), dma="R1")
            for kv in range(KVH):
                for tc in range(2):
                    pb = tc % 2
                    for k in range(KD):
                        rgk = rg[k // 8]
                        wsl = rgk[:, (k % 8) * 512 + kv * 128:
                                  (k % 8) * 512 + kv * 128 + 128]
                        MM(pb, ps[pb][:], wsl,
                           xn[:, k * TH + tc * 512:k * TH + (tc + 1) * 512],
                           start=(k == 0), stop=(k == KD - 1),
                           reads=(r_rg[k // 8], r_xn))
                    base = kv * 1024 + tc * 512
                    emit_rope(pb, 2 + (kv % 2), am[:, base:base + 512],
                              pos0 + tc * 512, r_kvst)
            b.op("g", lambda h: h.dma_start(
                rg[0][:].rearrange("p (k j) -> p k j", k=8), wv_d[:, 0:8]),
                writes=(r_rg[0],), dma="R0")
            b.op("g", lambda h: h.dma_start(
                rg[1][:].rearrange("p (k j) -> p k j", k=8), wv_d[:, 8:16]),
                writes=(r_rg[1],), dma="R1")
            for tt in range(8):
                pb = tt % 2
                for k in range(KD):
                    rgk = rg[k // 8]
                    MM(pb, ps[pb][:],
                       xn[:, k * TH + tt * 128:k * TH + (tt + 1) * 128],
                       rgk[:, (k % 8) * 512:(k % 8 + 1) * 512],
                       start=(k == 0), stop=(k == KD - 1),
                       reads=(r_rg[k // 8], r_xn))
                b.op("v", lambda h, tt=tt, pb=pb: h.tensor_copy(
                    am[:, 4096 + tt * 512:4096 + (tt + 1) * 512],
                    ps[pb][:]),
                    psum_reads=(r_ps[pb],), writes=(r_kvst,))
            b.op("y", lambda h, half=half: h.dma_start(kv_scr[half],
                                                       am[:, 0:8192]),
                 reads=(r_kvst,), writes=(r_kvscr,), dma="KV")
            merge(r_am, [r_kvst])

        def finish(buf, rbuf):
            b.op("g", lambda h: h.dma_start(
                out_d, buf[:].rearrange("p (k t) -> p k t", k=KD)),
                reads=(rbuf,), dma="OUT")
            b.op("g", None, extra=[("OUT", b.cnt["OUT"])])
            b.op("y", None, extra=[("OUT", b.cnt["OUT"])])
            emit_block()
            return nc

        def emit_block():
            with nc.Block() as blk:
                @blk.tensor
                def _(h):
                    for f in b.ops["t"]:
                        f(h)

                @blk.vector
                def _(h):
                    for f in b.ops["v"]:
                        f(h)

                @blk.scalar
                def _(h):
                    for f in b.ops["s"]:
                        f(h)

                @blk.gpsimd
                def _(h):
                    for f in b.ops["g"]:
                        f(h)

                @blk.sync
                def _(h):
                    for f in b.ops["y"]:
                        f(h)

        emit_ln(xp, r_xp)
        if dbg == "ln1":
            return finish(xn, r_xn)
        emit_router(dbg[3:] if dbg and dbg.startswith("rt_") else "full")
        if dbg and dbg.startswith("rt_"):
            return finish(xn, r_xn)
        if dbg == "rt":
            b.op("v", lambda h: h.tensor_copy(xn[0:E, 0:TH], ShT[:]),
                 reads=(r_ShT,), writes=(r_xn,))
            b.op("v", lambda h: h.tensor_copy(xn[:, 2048:2064], lg[:]),
                 reads=(r_lg,), writes=(r_xn,))
            b.op("v", lambda h: h.tensor_copy(xn[:, 2064:2068], m4[:]),
                 reads=(r_m4,), writes=(r_xn,))
            b.op("v", lambda h: h.tensor_copy(xn[:, 2068:2072], e4[:]),
                 reads=(r_e4,), writes=(r_xn,))
            b.op("v", lambda h: h.tensor_copy(xn[:, 2072:2076], w4[:]),
                 reads=(r_w4,), writes=(r_xn,))
            b.op("v", lambda h: h.tensor_copy(xn[:, 2076:2092], Sll[:]),
                 reads=(r_S,), writes=(r_xn,))
            b.op("v", lambda h: h.tensor_copy(xn[:, 2092:2156], msk4[:]),
                 reads=(r_msk4,), writes=(r_xn,))
            return finish(xn, r_xn)
        emit_experts_sparse(xp, r_xp, 0)
        r_wc.wl = [("S0", b.cnt["S0"]), ("S1", b.cnt["S1"])]
        if dbg == "ffn1p":
            return finish(xp, r_xp)
        emit_ln(xp, r_xp)
        emit_kv(0, 0)
        if dbg == "kv0":
            return finish(am, r_am)
        emit_ln(xo, r_xo)
        emit_router()
        emit_experts_sparse(xo, r_xo, 1)
        if dbg == "ffn1o":
            return finish(xo, r_xo)
        emit_ln(xo, r_xo)
        emit_kv(1, TH)

        r_vct = Res("vct")
        fork(r_am, [r_vct])

        def kc_ap(kv, col0, n):
            return rg[kv // 2][:, (kv % 2) * 2048 + col0:
                               (kv % 2) * 2048 + col0 + n]

        def vc_ap(tt, col0, n):
            if tt < 12:
                return xg[:, tt * 512 + col0:tt * 512 + col0 + n]
            return am[:, 12288 + (tt - 12) * 512 + col0:
                      12288 + (tt - 12) * 512 + col0 + n]

        r_vc = [r_xg, r_vct]
        for half in range(2):
            for kv in range(KVH):
                b.op("y", lambda h, half=half, kv=kv: h.dma_start(
                    kc_ap(kv, half * 1024, 1024),
                    kv_scr[half][:, kv * 1024:(kv + 1) * 1024]),
                    reads=(r_kvscr,), writes=(r_rg[kv // 2],), dma="KV")
            for tt2 in range(8):
                tt = half * 8 + tt2
                b.op("y", lambda h, half=half, tt=tt, tt2=tt2: h.dma_start(
                    vc_ap(tt, 0, 512),
                    kv_scr[half][:, 4096 + tt2 * 512:4096 + (tt2 + 1) * 512]),
                    reads=(r_kvscr,),
                    writes=(r_xg if tt < 12 else r_vct,), dma="KV")

        kvtot = [("KV", b.cnt["KV"])]
        for rr in (r_rg[0], r_rg[1], r_xg, r_vct):
            rr.wl = list(kvtot)
        ksq = am[:, 5120:7168]
        qsq = am[:, 7168:7680]
        wqa = [am[:, 8192:10240], am[:, 10240:12288]]
        pslot = [am[:, 0:2048], am[:, 2048:4096]]
        qslot = [am[:, 4096:4608], am[:, 4608:5120]]
        ptbuf = [xp[:, 0:2048], xp[:, 2048:4096]]
        ybuf = xp[:, 8192:16384]
        r_wqa = [Res("wqa0"), Res("wqa1")]
        r_pslot = [Res("p0"), Res("p1")]
        r_qslot = [Res("q0"), Res("q1")]
        r_ptbuf = [Res("pt0"), Res("pt1")]
        r_y = Res("y")
        fork(r_am, r_wqa + r_pslot + r_qslot)
        fork(r_xp, r_ptbuf + [r_y])

        for kv in range(KVH):
            b.op("s", lambda h, kv=kv: h.activation(
                ksq, kc_ap(kv, 0, 2048), AF.Square),
                reads=(r_rg[kv // 2],), writes=(r_am,))
            for c in range(4):
                MM(6, ps[6][:], ones128[:], ksq[:, c * 512:(c + 1) * 512],
                   start=True, stop=True, reads=(r_am, r_consts))
                dst = k2m[0:1, kv:kv + 1] if c == 0 else k2c[:]
                rdst = (r_k2m,) if c == 0 else (r_k2c,)
                b.op("v", lambda h, d=dst: h.reduce_max(
                    d, ps[6][0:1, :], axis=AX.X),
                    psum_reads=(r_ps[6],), writes=rdst)
                if c > 0:
                    b.op("v", lambda h, kv=kv: h.tensor_max(
                        k2m[0:1, kv:kv + 1], k2m[0:1, kv:kv + 1], k2c[:]),
                        reads=(r_k2c, r_k2m), writes=(r_k2m,))

        for tc in range(2):
            for hd in range(NH):
                kv = hd // 4
                ws = hd % 2
                b.op("g", lambda h, hd=hd, ws=ws: h.dma_start(
                    wqa[ws][:].rearrange("p (k j) -> p k j", k=KD), wq_d[hd]),
                    writes=(r_wqa[ws],), dma=f"Q{ws}")
                for k in range(KD):
                    MM(5, ps[5][:], wqa[ws][:, k * 128:(k + 1) * 128],
                       xn[:, k * TH + tc * 512:k * TH + (tc + 1) * 512],
                       start=(k == 0), stop=(k == KD - 1),
                       reads=(r_wqa[ws], r_xn))
                qs = qslot[hd % 2]
                r_qs = r_qslot[hd % 2]
                emit_rope(5, 2 + (hd % 2), qs, TH + tc * 512, r_qs)
                b.op("s", lambda h, q=qs: h.activation(qsq, q, AF.Square),
                     reads=(r_qs,), writes=(r_am,))
                MM(6, ps[6][:], ones128[:], qsq, start=True, stop=True,
                   reads=(r_am, r_consts))
                b.op("v", lambda h: h.reduce_max(q2m[:], ps[6][0:1, :],
                                                 axis=AX.X),
                     psum_reads=(r_ps[6],), writes=(r_q2m,))
                b.op("v", lambda h, kv=kv: h.tensor_mul(
                    b2[:], q2m[:], k2m[0:1, kv:kv + 1]),
                    reads=(r_q2m, r_k2m), writes=(r_b2,))
                b.op("s", lambda h: h.activation(lnb2[:], b2[:], AF.Ln),
                     reads=(r_b2,), writes=(r_lnb2,))
                b.op("s", lambda h: h.activation(bsq[:], lnb2[:], AF.Exp,
                                                 scale=0.5),
                     reads=(r_lnb2,), writes=(r_bsq,))
                b.op("v", lambda h: h.tensor_scalar_mul(nbv[:], bsq[:], -SCL),
                     reads=(r_bsq,), writes=(r_nbv,))
                b.op("v", lambda h: h.tensor_copy(nbb[:], nbv[:]),
                     reads=(r_nbv,), writes=(r_nbb,))
                MM(6, ps[6][:, 0:1], ones1[:], nbb[:], start=True, stop=True,
                   reads=(r_consts, r_nbb))
                b.op("v", lambda h: h.tensor_copy(bc[:], ps[6][:, 0:1]),
                     psum_reads=(r_ps[6],), writes=(r_bc,))
                b.op("v", lambda h: h.tensor_add(bc01[:], bc[:], c01[:]),
                     reads=(r_bc, r_consts), writes=(r_bc01,))
                for qt in range(4):
                    qtg = tc * 4 + qt
                    pcur = pslot[qt % 2]
                    r_pcur = r_pslot[qt % 2]
                    nkc = 4 if tc == 1 else 3
                    for kc in range(nkc):
                        pb = kc % 2
                        MM(pb, ps[pb][:], qs[:, qt * 128:(qt + 1) * 128],
                           kc_ap(kv, kc * 512, 512),
                           start=True, stop=True,
                           reads=(r_qs, r_rg[kv // 2]))
                        diag = (kc == 2 and qtg < 4) or (kc == 3 and qtg >= 4)
                        if diag:
                            sdx = qtg if qtg < 4 else qtg - 4
                            b.op("v", lambda h, pb=pb, s=sdx: h.tensor_tensor(
                                ps[pb][:], ps[pb][:],
                                tri[:, s * 512:(s + 1) * 512], ALU.add),
                                reads=(r_consts,), psum_reads=(r_ps[pb],),
                                writes=(r_ps[pb],))
                        bcol = bc01 if kc < 2 else bc
                        rbcol = r_bc01 if kc < 2 else r_bc
                        b.op("s", lambda h, pb=pb, kc=kc, pc=pcur, bco=bcol:
                             h.activation(pc[:, kc * 512:(kc + 1) * 512],
                                          ps[pb][:], AF.Exp, bias=bco[:],
                                          scale=SCL,
                                          accum_out=sums4[:, kc:kc + 1]),
                             psum_reads=(r_ps[pb],), reads=(rbcol,),
                             writes=(r_pcur, r_sums4))
                    b.op("v", lambda h: h.tensor_add(sumt[:], sums4[:, 0:1],
                                                     sums4[:, 1:2]),
                         reads=(r_sums4,), writes=(r_sumt,))
                    b.op("v", lambda h: h.tensor_add(sumt[:], sumt[:],
                                                     sums4[:, 2:3]),
                         reads=(r_sums4,), writes=(r_sumt,))
                    if nkc == 4:
                        b.op("v", lambda h: h.tensor_add(
                            sumt[:], sumt[:], sums4[:, 3:4]),
                            reads=(r_sums4,), writes=(r_sumt,))
                    b.op("v", lambda h: h.reciprocal(rst[:], sumt[:]),
                         reads=(r_sumt,), writes=(r_rst,))
                    dcur = Dt[qt % 2]
                    r_dcur = r_Dt[qt % 2]
                    b.op("v", lambda h, d=dcur: h.tensor_scalar(
                        d[:], ident[:], rst[:, 0:1], None, ALU.mult),
                        reads=(r_consts,), precise=(r_rst,), writes=(r_dcur,))
                    nks = 16 if tc == 1 else 12
                    ptc = ptbuf[qt % 2]
                    r_ptc = r_ptbuf[qt % 2]
                    for ksg in range(nks // 4):
                        pb = 2 + ksg % 2
                        for k4 in range(4):
                            ks = ksg * 4 + k4
                            MM(pb, ps[pb][:, k4 * 128:(k4 + 1) * 128],
                               pcur[:, ks * 128:(ks + 1) * 128], dcur[:],
                               start=True, stop=True,
                               reads=(r_pcur, r_dcur))
                        b.op("v", lambda h, pb=pb, ksg=ksg, pt=ptc:
                             h.tensor_copy(pt[:, ksg * 512:(ksg + 1) * 512],
                                           ps[pb][:]),
                             psum_reads=(r_ps[pb],), writes=(r_ptc,))
                    for ks in range(nks):
                        MM(4, ps[4][:, qt * 128:(qt + 1) * 128],
                           vc_ap(ks, kv * 128, 128),
                           ptc[:, ks * 128:(ks + 1) * 128],
                           start=(ks == 0), stop=(ks == nks - 1),
                           reads=(r_vc[0] if ks < 12 else r_vc[1], r_ptc))
                b.op("v", lambda h, hd=hd: h.tensor_copy(
                    ybuf[:, hd * 512:(hd + 1) * 512], ps[4][:]),
                    psum_reads=(r_ps[4],), writes=(r_y,))
            for dot in range(KD):
                ws = dot % 2
                b.op("g", lambda h, dot=dot, ws=ws: h.dma_start(
                    wqa[ws][:].rearrange("p (k j) -> p k j", k=KD), wo_d[dot]),
                    writes=(r_wqa[ws],), dma=f"Q{ws}")
                pb = 6 + dot % 2
                for k in range(KD):
                    MM(pb, ps[pb][:], wqa[ws][:, k * 128:(k + 1) * 128],
                       ybuf[:, k * 512:(k + 1) * 512],
                       start=(k == 0), stop=(k == KD - 1),
                       reads=(r_wqa[ws], r_y))
                xsl = xo[:, dot * TH + tc * 512:dot * TH + (tc + 1) * 512]
                b.op("v", lambda h, x=xsl, pb=pb: h.tensor_tensor(
                    x, x, ps[pb][:], ALU.add),
                    reads=(r_xo,), psum_reads=(r_ps[pb],), writes=(r_xo,))

        if dbg == "attnp":
            b.op("v", lambda h: h.tensor_copy(am[:, 0:8192], xp[:, 8192:16384]),
                 reads=(r_y,), writes=(r_am,))
            return finish(am, r_am)
        merge(r_am, r_wqa + r_pslot + r_qslot + [r_vct])
        merge(r_xp, r_ptbuf + [r_y])

        if dbg == "attn":
            return finish(xo, r_xo)
        emit_ln(xo, r_xo)
        emit_router()
        emit_experts_sparse(xo, r_xo, 2)
        return finish(xo, r_xo)



def _prep_shared(Wq, Wk, Wv, Wo, Wr, Rb, Wgu, Bgu, Wd, Bd):
    wq_h = np.ascontiguousarray(
        Wq.reshape(16, 128, 16, 128).transpose(2, 1, 0, 3))
    wo_h = np.ascontiguousarray(
        Wo.reshape(16, 128, 16, 128).transpose(2, 1, 0, 3))
    wk_h = np.ascontiguousarray(Wk.reshape(16, 128, 512).transpose(1, 0, 2))
    wv_h = np.ascontiguousarray(Wv.reshape(16, 128, 512).transpose(1, 0, 2))
    wr_h = np.ascontiguousarray(Wr.T.reshape(16, 128, E).transpose(1, 0, 2))
    rb_h = np.ascontiguousarray(Rb.reshape(1, E))
    gu = np.concatenate([Wgu[:, :, 0::2], Wgu[:, :, 1::2]], axis=2)
    wgu_h = np.ascontiguousarray(
        gu.reshape(E, 16, 128, 2, 16, 128).transpose(0, 2, 4, 3, 1, 5))
    bgu_de = np.concatenate([Bgu[:, 0::2], Bgu[:, 1::2]], axis=1)
    bgu_h = np.ascontiguousarray(bgu_de.reshape(E, 32, 128).transpose(2, 0, 1))
    wd_h = np.ascontiguousarray(
        Wd.reshape(E, 16, 128, 8, 256).transpose(0, 2, 3, 1, 4))
    bd_h = np.ascontiguousarray(Bd)
    return dict(wq=wq_h, wk=wk_h, wv=wv_h, wo=wo_h, wr=wr_h, rb=rb_h,
                wgu=wgu_h, bgu=bgu_h, wd=wd_h, bd=bd_h)


def _xt_tiles(x_half):
    return np.ascontiguousarray(
        x_half.T.reshape(16, 128, 1024).transpose(1, 0, 2))


def _sincos(h):
    inv = (1.0 / np.power(np.float32(10000.0),
                          np.arange(HALF, dtype=np.float32) /
                          np.float32(HALF))).astype(np.float32)
    par = (1 - h) * TH + np.arange(TH, dtype=np.float32)
    own = h * TH + np.arange(TH, dtype=np.float32)
    pos = np.concatenate([par, own]).astype(np.float32)
    ang = (pos[:, None] * inv[None, :]).astype(np.float32)
    s = np.sin(ang).astype(np.float32).T
    c = np.cos(ang).astype(np.float32).T
    sgn = np.concatenate([-s, s], axis=0)
    cc = np.concatenate([c, c], axis=0)
    return (np.ascontiguousarray(sgn).astype(bfd),
            np.ascontiguousarray(cc).astype(bfd))


_NC_CACHE = None


def kernel(L0_attn_x, L0_attn_Wq, L0_attn_Wk, L0_attn_Wv, L0_attn_Wo,
           L0_ffn_router_weight, L0_ffn_router_bias, L0_ffn_gate_up_proj,
           L0_ffn_gate_up_proj_bias, L0_ffn_down_proj, L0_ffn_down_proj_bias):
    global _NC_CACHE
    x = np.asarray(L0_attn_x, np.float32)
    shared = _prep_shared(
        np.asarray(L0_attn_Wq, np.float32), np.asarray(L0_attn_Wk, np.float32),
        np.asarray(L0_attn_Wv, np.float32), np.asarray(L0_attn_Wo, np.float32),
        np.asarray(L0_ffn_router_weight, np.float32),
        np.asarray(L0_ffn_router_bias, np.float32),
        np.asarray(L0_ffn_gate_up_proj, np.float32),
        np.asarray(L0_ffn_gate_up_proj_bias, np.float32),
        np.asarray(L0_ffn_down_proj, np.float32),
        np.asarray(L0_ffn_down_proj_bias, np.float32))
    in_maps = []
    for c in range(8):
        bq, h = divmod(c, 2)
        own = x[bq, h * TH:(h + 1) * TH]
        par = x[bq, (1 - h) * TH:(2 - h) * TH]
        sin_h, cos_h = _sincos(h)
        m = dict(shared)
        m["xo"] = _xt_tiles(own)
        m["xp"] = _xt_tiles(par)
        m["sin"] = sin_h
        m["cos"] = cos_h
        m["c01"] = np.full((128, 1), 0.0 if h == 1 else NEG, np.float32)
        in_maps.append(m)
    if _NC_CACHE is None:
        _NC_CACHE = build_program()
    res = run_bass_kernel_spmd(_NC_CACHE, in_maps, core_ids=list(range(8)))
    out = np.empty((4, 2048, 2048), np.float32)
    for c in range(8):
        bq, h = divmod(c, 2)
        arr = np.asarray(res.results[c]["out"])
        out[bq, h * TH:(h + 1) * TH] = arr.transpose(1, 0, 2).reshape(2048, TH).T
    return out


if __name__ == "__main__":
    nci = build_program()
    print("instructions:", nci.n_instructions())
